# version 21
# baseline (speedup 1.0000x reference)
import os
import contextlib
import numpy as np
import ml_dtypes
import concourse.bass as bass
import concourse.mybir as mybir
from concourse import dve_ops
from concourse.bass_utils import run_bass_kernel_spmd

F32 = mybir.dt.float32
BF16 = mybir.dt.bfloat16
AF = mybir.ActivationFunctionType
ALU = mybir.AluOpType
AX = mybir.AxisListType

HID = 4096
NF = 32
D = 128
NQ = 32
NKV = 8
G = 4
TOWN = 1024
TALL = 2048
SM = float(D) ** -0.5
EPS = 1e-5
NR = 8
INTER = 11008
NI = 86
NH2 = 172
GSZ = 8
NEG = -1e30
THETA = 10000.0

DEV_STOP = os.environ.get("MKDEV_STOP", "")

STRIPS = []
_eo = 0
for _st in range(16):
    _qlo = max(0, (_st - 8) * 128)
    _qhi = min(TOWN, (_st + 1) * 128)
    STRIPS.append((_qlo, _qhi, _eo))
    _eo += _qhi - _qlo
E_TOT = _eo


def _pieces(lo, hi, step=512):
    out, p = [], lo
    while p < hi:
        p2 = min(hi, (p // step + 1) * step)
        out.append((p, p2))
        p = p2
    return out


def _ffn_groups():
    g, i = [], 0
    while i < NI:
        g.append((i, min(NI, i + GSZ)))
        i += GSZ
    return g


class Sem:
    __slots__ = ("h", "n", "name")


class Prog:
    ENGS = ("sync", "gpsimd", "tensor", "scalar", "vector")

    def __init__(self, nc, es):
        self.nc = nc
        self.es = es
        self.q = {e: [] for e in self.ENGS}
        self.waited = {e: {} for e in self.ENGS}
        self.auto_sems = {}

    def sem(self, name):
        s = Sem()
        s.h = self.es.enter_context(self.nc.semaphore(name))
        s.n = 0
        s.name = name
        return s

    def add(self, eng, fn, waits=(), inc=None, inck=1):
        wl = []
        wd = self.waited[eng]
        auto = self.auto_sems.get(eng)
        if auto is not None:
            if auto.n > 0:
                wl.append((auto.h, auto.n))
                wd[auto.name] = max(wd.get(auto.name, 0), auto.n)
            if inc is None:
                inc = auto
        for sv in waits:
            if sv is None:
                continue
            s, v = sv
            if v is None or v <= 0:
                continue
            if v > wd.get(s.name, 0):
                wl.append((s.h, v))
                wd[s.name] = v
        tgt = None
        if inc is not None:
            if inck == 16 and inc.n > 0:
                wl.append((inc.h, inc.n))
                wd[inc.name] = max(wd.get(inc.name, 0), inc.n)
            inc.n += inck
            tgt = inc.n

        def op(e, fn=fn, wl=wl, inc=inc, inck=inck):
            for (h, v) in wl:
                e.wait_ge(h, v)
            ins = fn(e)
            if inc is not None:
                ins.then_inc(inc.h, inck)

        self.q[eng].append(op)
        return tgt

    def flush(self):
        qs = {k: v for k, v in self.q.items() if v}
        if not qs:
            return
        with self.nc.Block() as blk:
            for name, ops in qs.items():
                def run(e, ops=ops):
                    for op in ops:
                        op(e)
                getattr(blk, name)(run)
        self.q = {e: [] for e in self.ENGS}


def _blk(w, kc, nc_):
    k, n = w.shape
    assert k == kc * 128 and n == nc_ * 128
    return np.ascontiguousarray(w.reshape(kc, 128, nc_, 128).transpose(2, 0, 1, 3))


def build_program():
    nc = bass.Bass("TRN2", target_bir_lowering=False)

    x_h = nc.dram_tensor("x_rot", [HID, TALL], F32, kind="ExternalInput")
    wq_h = nc.dram_tensor("wq_b", [NQ, NF, 128, 128], F32, kind="ExternalInput")
    wk_h = nc.dram_tensor("wk_b", [NKV, NF, 128, 128], F32, kind="ExternalInput")
    wv_h = nc.dram_tensor("wv_b", [NKV, NF, 128, 128], F32, kind="ExternalInput")
    wo_h = nc.dram_tensor("wo_b", [NF, NQ, 128, 128], F32, kind="ExternalInput")
    u1_h = nc.dram_tensor("u1_b", [NR, NF, 128, 128], F32, kind="ExternalInput")
    v1_h = nc.dram_tensor("v1_b", [NI, 2, NR, 128, 128], F32, kind="ExternalInput")
    u2_h = nc.dram_tensor("u2_b", [NI * NR, 128, 128], F32, kind="ExternalInput")
    v2_h = nc.dram_tensor("v2_b", [NF, NR, 128, 128], F32, kind="ExternalInput")
    b1_h = nc.dram_tensor("b1_t", [128, NH2], F32, kind="ExternalInput")
    b2_h = nc.dram_tensor("b2_t", [128, NF], F32, kind="ExternalInput")
    cc_h = nc.dram_tensor("cc_t", [128, TALL], BF16, kind="ExternalInput")
    ss_h = nc.dram_tensor("ss_t", [128, TALL], BF16, kind="ExternalInput")
    diag_h = nc.dram_tensor("diag_m", [128, 128], BF16, kind="ExternalInput")
    win_h = nc.dram_tensor("win_m", [128, 128], BF16, kind="ExternalInput")
    psw_h = nc.dram_tensor("pswap", [128, 128], BF16, kind="ExternalInput")
    idn_h = nc.dram_tensor("ident", [128, 128], BF16, kind="ExternalInput")
    boff_h = nc.dram_tensor("bias_off", [1, 32], F32, kind="ExternalInput")
    onr_h = nc.dram_tensor("ones_row", [1, 128], F32, kind="ExternalInput")
    onc_h = nc.dram_tensor("ones_col", [128, 1], BF16, kind="ExternalInput")

    out_h = nc.dram_tensor("out", [HID, TOWN], F32, kind="ExternalOutput")
    q_d = nc.dram_tensor("q_bounce", [NQ, 128, TOWN], BF16, kind="Internal")
    rec_d = nc.dram_tensor("rec_bounce", [2, 1024], BF16, kind="Internal")
    a_d = nc.dram_tensor("a_bounce", [NF, 128, TOWN], BF16, kind="Internal")
    x1_d = nc.dram_tensor("x1_bounce", [NF, 128, TOWN], BF16, kind="Internal")

    es = contextlib.ExitStack()
    with es:
        P = Prog(nc, es)

        def sb(name, shape, dt, st=None):
            return (st or es).enter_context(nc.sbuf_tensor(name, shape, dt))

        dG = P.sem("dG")
        dX = [P.sem("dX0"), P.sem("dX1"), P.sem("dX2"), P.sem("dX3")]
        dG0 = P.sem("dG0")
        dG1 = P.sem("dG1")
        dL = P.sem("dL")
        dL0 = P.sem("dL0")
        dL1 = P.sem("dL1")
        dO0 = P.sem("dO0")
        dO1 = P.sem("dO1")
        dT0 = P.sem("dT0")
        dT1 = P.sem("dT1")
        sT = P.sem("sT")
        sS = P.sem("sS")
        sV = P.sem("sV")
        sG = P.sem("sG")
        P.auto_sems = {"scalar": sS, "vector": sV, "gpsimd": sG}
        P.dma_sems = [dG, dG0, dG1, dL, dL0, dL1, dO0, dO1, dT0, dT1] + dX

        diag = sb("diag", [128, 128], BF16)
        win = sb("win", [128, 128], BF16)
        psw = sb("psw", [128, 128], BF16)
        idn = sb("idn", [128, 128], BF16)
        onc = sb("onc", [128, 1], BF16)
        onr = sb("onr", [1, 128], F32)
        onrb = sb("onrb", [1, 128], BF16)
        b1t = sb("b1t", [128, NH2], F32)
        b2t = sb("b2t", [128, NF], F32)
        boff = sb("boff", [1, 32], F32)
        btabA = sb("btabA", [128, 32], F32)
        btabB = sb("btabB", [128, 32], F32)
        m_row = sb("m_row", [1, TOWN], F32)
        v_row = sb("v_row", [1, TOWN], F32)
        r_row = sb("r_row", [1, TOWN], F32)
        mr_row = sb("mr_row", [1, TOWN], F32)
        rec_row = sb("rec_row", [1, TOWN], F32)
        r_bf = sb("r_bf", [1, TOWN], BF16)
        mr_bf = sb("mr_bf", [1, TOWN], BF16)
        rec_bf = mr_bf
        rec_scr = v_row
        qmax_row = sb("qmax_row", [1, NQ], F32)
        kmax_row = sb("kmax_row", [1, NKV], F32)
        nt = sb("nt", [1, 4], F32)
        srow = sb("srow", [1, 256], F32)
        grR = sb("grR", [128, TOWN], BF16)
        grMR = sb("grMR", [128, TOWN], BF16)
        epsc = sb("epsc", [128, 1], F32)
        z512 = sb("z512", [128, 512], BF16)
        tinyc = sb("tinyc", [128, 1], F32)

        PSA = es.enter_context(nc.psum_tensor("psa", [128, 3584], F32))
        MMS = [1024, 2048]
        STATS = 3072
        SWAP = 0

        _cl = [(onc, onc_h), (diag, diag_h), (win, win_h),
               (psw, psw_h), (idn, idn_h)]
        _csems = [dL, dO0, dO1, dT0, dT1]
        consts_w = []
        for i, (t, h) in enumerate(_cl + [(onr, onr_h), (b1t, b1_h), (b2t, b2_h), (boff, boff_h)]):
            s = _csems[i % len(_csems)]
            v = P.add("sync", lambda e, t=t, h=h: e.dma_start(t[:], h.ap()), inc=s, inck=16)
            consts_w.append((s, v))
        consts_w = [(s, max(v for s2, v in consts_w if s2 is s)) for s in _csems]
        mz1 = P.add("vector", lambda e: e.memset(epsc[:], EPS), inc=sV)
        P.add("vector", lambda e: e.memset(onrb[:], 1.0), inc=sV)
        mz0 = P.add("vector", lambda e: e.memset(z512[:], 0.0), inc=sV)
        mz2 = P.add("vector", lambda e: e.memset(tinyc[:], 1e-30), inc=sV)

        s1 = contextlib.ExitStack()
        es.enter_context(s1)
        k_fm = sb("k_fm", [128, NKV * TALL], BF16, s1)
        v_tm = sb("v_tm", [128, 16 * 1024], BF16, s1)

        sA = contextlib.ExitStack()
        es.enter_context(sA)
        y1 = sb("y1", [128, NF * 1024], BF16, sA)
        cc = sb("cc", [128, TALL], BF16, sA)
        ssn = sb("ssn", [128, TALL], BF16, sA)
        _ccv = P.add("sync", lambda e: e.dma_start(cc[:], cc_h.ap()), inc=dT0, inck=16)
        _ssv = P.add("sync", lambda e: e.dma_start(ssn[:], ss_h.ap()), inc=dT1, inck=16)
        P.add("vector", lambda e: e.nop(),
              waits=[(dT0, _ccv), (dT1, _ssv)] + consts_w)
        sbf = sb("sbf", [128, 3 * 2048], BF16, sA)
        xstg = {0: sbf[:, 2048:4096].bitcast(F32),
                1: sbf[:, 4096:6144].bitcast(F32)}
        xstgrel = {0: [], 1: []}
        scf = sb("scf", [128, 1024], F32, sA)
        scfb = sb("scfb", [128, 2 * 1024], BF16, sA)
        wbuf = sb("wbufA", [128, 2 * 4096], BF16, sA)
        PSB = sA.enter_context(nc.psum_tensor("psb", [128, 1024], BF16))
        P.add("tensor", lambda e: e.nop(), waits=consts_w + [(sV, mz2)])
        kv_pre = {}
        WSL = 4096

        def wsl(slot, j):
            return wbuf[:, slot * WSL + j * 128: slot * WSL + (j + 1) * 128]

        wrel = {0: [], 1: []}
        psrel = {0: [], 1: []}
        strel = {0: [], 1: []}
        sqrel = {0: [], 1: []}
        rorel = {0: [], 1: []}
        swaprel = []
        vtmrel = []
        nrmrel = []
        ntrel = []
        qstore = []

        def phase_A_half(half, do_q):
            base = half * 1024
            xwait = {}
            for fc in range(NF):
                src = x_h.ap()[fc * 128:(fc + 1) * 128, base:base + 1024]
                if fc % 2 == 0:
                    xs = dX[(fc // 2) % 4]
                    xwait[fc] = (xs, P.add("gpsimd", lambda e, fc=fc, src=src:
                          e.dma_start(y1[:, fc * 1024:(fc + 1) * 1024], src), inc=xs, inck=16))
                else:
                    ssl = (fc // 2) % 2
                    guard = strel[0] + strel[1] if ssl == 0 else sqrel[0] + sqrel[1]
                    lsem = dL if ssl == 0 else dO0
                    ld = P.add("sync", lambda e, src=src, ssl=ssl:
                               e.dma_start(xstg[ssl], src),
                               waits=xstgrel[ssl] + guard, inc=lsem, inck=16)
                    cpv = P.add("scalar", lambda e, fc=fc, ssl=ssl:
                                e.copy(y1[:, fc * 1024:(fc + 1) * 1024], xstg[ssl]),
                                waits=[(lsem, ld)], inc=sS)
                    xstgrel[ssl] = [(sS, cpv)]
                    xwait[fc] = (sS, cpv)
            if half == 0:
                for c in (0, 1):
                    wsrc = wk_h.ap()[c].rearrange("f p c -> p f c")
                    ssem = dG0 if c == 0 else dG1
                    kv_pre[c] = P.add("gpsimd", lambda e, wsrc=wsrc, c=c:
                                      e.dma_start(wbuf[:, c * WSL:(c + 1) * WSL]
                                                  .rearrange("p (f c) -> p f c", c=128), wsrc),
                                      inc=ssem, inck=16)
            sqv = {}
            mmv = {}
            for fc in range(NF):
                sl = fc % 2
                sqap = sbf[:, sl * 1024:(sl + 1) * 1024]
                sqv[fc] = P.add("vector", lambda e, fc=fc, sqap=sqap:
                                e.tensor_tensor(sqap, y1[:, fc * 1024:(fc + 1) * 1024],
                                                y1[:, fc * 1024:(fc + 1) * 1024], ALU.mult),
                                waits=[xwait[fc]] + rorel[sl], inc=sV)
                last = None
                for (p0, p1) in _pieces(0, 1024):
                    P.add("tensor", lambda e, fc=fc, p0=p0, p1=p1:
                          e.matmul(PSA[0:1, p0:p1], onc[:],
                                   y1[:, fc * 1024 + p0:fc * 1024 + p1],
                                   start=(fc == 0), stop=(fc == NF - 1)),
                          waits=[xwait[fc]], inc=None)
                    last = P.add("tensor", lambda e, sqap=sqap, p0=p0, p1=p1, fc=fc:
                                 e.matmul(PSA[0:1, 1024 + p0:1024 + p1], onc[:], sqap[:, p0:p1],
                                          start=(fc == 0), stop=(fc == NF - 1)),
                                 waits=[(sV, sqv[fc])], inc=sT)
                mmv[fc] = last
                rorel[sl] = [(sT, last)]
            stats_mm = mmv[NF - 1]
            a1 = P.add("scalar", lambda e: e.activation(m_row[:], PSA[0:1, 0:1024], AF.Copy, scale=1.0 / HID),
                       waits=[(sT, stats_mm)], inc=sS)
            a2 = P.add("scalar", lambda e: e.activation(v_row[:], PSA[0:1, 1024:2048], AF.Copy, scale=1.0 / HID),
                       inc=sS)
            t1 = P.add("vector", lambda e: e.tensor_tensor(mr_row[:], m_row[:], m_row[:], ALU.mult),
                       waits=[(sS, a2)], inc=sV)
            t2 = P.add("vector", lambda e: e.tensor_tensor(v_row[:], v_row[:], mr_row[:], ALU.subtract),
                       waits=[(sV, t1)], inc=sV)
            l1 = P.add("scalar", lambda e: e.activation(r_row[:], v_row[:], AF.Ln, bias=epsc[0:1, :]),
                       waits=[(sV, t2)] + consts_w, inc=sS)
            l2 = P.add("scalar", lambda e: e.activation(r_row[:], r_row[:], AF.Exp, scale=-0.5),
                       waits=[(sS, l1)], inc=sS)
            t3 = P.add("vector", lambda e: e.tensor_tensor(mr_row[:], m_row[:], r_row[:], ALU.mult),
                       waits=[(sS, l2)], inc=sV)
            stats_read = a2
            bcv = None
            for (p0, p1) in _pieces(0, 1024):
                P.add("tensor", lambda e, p0=p0, p1=p1:
                      e.matmul(PSA[:, 1024 + p0:1024 + p1], onr[:], r_row[:, p0:p1],
                               start=True, stop=True),
                      waits=[(sS, l2)] + psrel[0] + psrel[1], inc=None)
                bcv = P.add("tensor", lambda e, p0=p0, p1=p1:
                      e.matmul(PSA[:, 2048 + p0:2048 + p1], onr[:], mr_row[:, p0:p1],
                               start=True, stop=True),
                      waits=[(sV, t3)], inc=sT)
            g1 = P.add("scalar", lambda e: e.copy(grR[:], PSA[:, 1024:2048]),
                       waits=[(sT, bcv)], inc=sS)
            g2 = P.add("scalar", lambda e: e.copy(grMR[:], PSA[:, 2048:3072]), inc=sS)
            psrel[0] = [(sS, g1)]
            psrel[1] = [(sS, g2)]
            apv = {}
            for fc in range(NF):
                ch = y1[:, fc * 1024:(fc + 1) * 1024]
                sl = fc % 2
                sc = scfb[:, sl * 1024:(sl + 1) * 1024]
                w = [(sS, g2), (sV, sqv[fc]), (sT, mmv[fc])]
                if fc >= 2:
                    w.append((sV, apv[fc - 2]))
                P.add("vector", lambda e, ch=ch, sc=sc: e.tensor_tensor(sc, ch, grR[:], ALU.mult),
                      waits=w, inc=sV)
                apv[fc] = P.add("vector", lambda e, ch=ch, sc=sc:
                                e.tensor_tensor(ch, sc, grMR[:], ALU.subtract), inc=sV)
            y1d = apv[NF - 1]
            scfrel = [(sV, y1d)]

            iters = [("kv", c) for c in range(16)] + ([("q", i) for i in range(NQ)] if do_q else [])
            deferred = []
            nonlocal_state = {}

            for it_i, (kind, idx) in enumerate(iters):
                sl = it_i % 2
                ssem = dG0 if sl == 0 else dG1
                if kind == "kv":
                    wsrc = (wk_h if idx < 8 else wv_h).ap()[idx % 8].rearrange("f p c -> p f c")
                else:
                    wsrc = wq_h.ap()[idx].rearrange("f p c -> p f c")
                if half == 0 and it_i < 2:
                    wld = kv_pre[it_i]
                else:
                    wld = P.add("gpsimd", lambda e, wsrc=wsrc, sl=sl:
                                e.dma_start(wbuf[:, sl * WSL:(sl + 1) * WSL]
                                            .rearrange("p (f c) -> p f c", c=128), wsrc),
                                waits=wrel[sl], inc=ssem, inck=16)
                wrel[sl] = []
                pb = MMS[sl]
                pw = [(ssem, wld)] + psrel[sl]
                mml = None
                for fc in range(NF):
                    fw = list(pw) + ([(sV, apv[fc])] if it_i < 2 else [])
                    for (p0, p1) in _pieces(0, 1024):
                        mml = P.add("tensor",
                                    lambda e, fc=fc, p0=p0, p1=p1, pb=pb, sl=sl:
                                    e.matmul(PSA[:, pb + p0:pb + p1], wsl(sl, fc),
                                             y1[:, fc * 1024 + p0:fc * 1024 + p1],
                                             start=(fc == 0), stop=(fc == NF - 1)),
                                    waits=fw,
                                    inc=sT if (fc == NF - 1 and p1 == 1024) else None)
                        fw = ()
                        pw = ()
                wrel[sl] = [(sT, mml)]
                for dfn in deferred:
                    dfn()
                deferred = []
                stage = sbf[:, 2048 + sl * 1024:2048 + (sl + 1) * 1024]
                ev = P.add("scalar", lambda e, stage=stage, pb=pb: e.copy(stage, PSA[:, pb:pb + 1024]),
                           waits=[(sT, mml)] + strel[sl], inc=sS)
                psrel[sl] = [(sS, ev)]
                if kind == "kv" and idx >= 8:
                    c8 = idx - 8
                    def dv(stage=stage, c8=c8, half=half, ev=ev, sl=sl):
                        tpw = [(sS, ev)] + vtmrel[-1:] if vtmrel else [(sS, ev)]
                        tl = None
                        for tt in range(8):
                            tl = P.add("tensor", lambda e, stage=stage, tt=tt:
                                       e.transpose(PSB[:, tt * 128:(tt + 1) * 128],
                                                   stage[:, tt * 128:(tt + 1) * 128], idn[:]),
                                       waits=tpw if tt == 0 else (), inc=sT)
                            tpw = ()
                        vout = v_tm.ap().rearrange("p (t h d) -> p t h d", t=16, h=8, d=128)[
                            :, half * 8:half * 8 + 8, c8, :]
                        vin = PSB.ap().rearrange("p (t d) -> p t d", t=8, d=128)
                        vt = P.add("scalar", lambda e, vout=vout, vin=vin: e.copy(vout, vin),
                                   waits=[(sT, tl)], inc=sS)
                        vtmrel.append((sS, vt))
                        strel[sl] = [(sT, tl)]
                    deferred.append(dv)
                else:
                    sqs = sbf[:, 4096 + sl * 1024:4096 + (sl + 1) * 1024]
                    sq = P.add("scalar", lambda e, stage=stage, sqs=sqs:
                               e.activation(sqs, stage, AF.Square),
                               waits=sqrel[sl], inc=sS)
                    sqrel[sl] = []

                    def dk(stage=stage, sqs=sqs, sq=sq, kind=kind, idx=idx, sl=sl,
                           ev=ev, base=base, half=half):
                        p0v = P.add("tensor", lambda e, sqs=sqs:
                                    e.matmul(PSA[0:1, STATS:STATS + 512], onc[:],
                                             sqs[:, 0:512], start=True, stop=True),
                                    waits=[(sS, sq)] + nrmrel[-1:], inc=sT)
                        sww = [(sS, ev)] + swaprel[-1:]
                        swl = None
                        for (p0, p1) in _pieces(0, 1024):
                            swl = P.add("tensor", lambda e, stage=stage, p0=p0, p1=p1:
                                        e.matmul(PSA[:, SWAP + p0:SWAP + p1], psw[:],
                                                 stage[:, p0:p1], start=True, stop=True),
                                        waits=sww, inc=sT)
                            sww = ()
                        r0 = P.add("vector", lambda e:
                                   e.reduce_max(nt[0:1, 0:1], PSA[0:1, STATS:STATS + 512], axis=AX.X),
                                   waits=[(sT, p0v)] + ntrel[-1:], inc=sV)
                        p1v = P.add("tensor", lambda e, sqs=sqs:
                                    e.matmul(PSA[0:1, STATS:STATS + 512], onc[:],
                                             sqs[:, 512:1024], start=True, stop=True),
                                    waits=[(sV, r0)], inc=sT)
                        r1 = P.add("vector", lambda e:
                                   e.reduce_max(nt[0:1, 1:2], PSA[0:1, STATS:STATS + 512], axis=AX.X),
                                   waits=[(sT, p1v)], inc=sV)
                        nrmrel.append((sV, r1))
                        mx = P.add("vector", lambda e:
                                   e.tensor_tensor(nt[0:1, 2:3], nt[0:1, 0:1], nt[0:1, 1:2], ALU.max),
                                   inc=sV)
                        if kind == "kv":
                            if half == 0:
                                nv = P.add("vector", lambda e, idx=idx:
                                           e.tensor_copy(kmax_row[0:1, idx:idx + 1], nt[0:1, 2:3]),
                                           waits=[(sV, mx)], inc=sV)
                            else:
                                nv = P.add("vector", lambda e, idx=idx:
                                           e.tensor_tensor(kmax_row[0:1, idx:idx + 1],
                                                           kmax_row[0:1, idx:idx + 1], nt[0:1, 2:3], ALU.max),
                                           waits=[(sV, mx)], inc=sV)
                        else:
                            nv = P.add("vector", lambda e, idx=idx:
                                       e.tensor_copy(qmax_row[0:1, idx:idx + 1], nt[0:1, 2:3]),
                                       waits=[(sV, mx)], inc=sV)
                        ntrel.append((sV, nv))
                        sqrel[sl] = [(sT, p1v)]
                        if kind == "kv":
                            dst = k_fm[:, idx * TALL + base: idx * TALL + base + 1024]
                            tb = base
                            rowait = []
                        else:
                            dst = sbf[:, sl * 1024:(sl + 1) * 1024]
                            tb = 1024
                            rowait = rorel[sl]
                        m1 = P.add("vector", lambda e, stage=stage, tb=tb:
                                   e.tensor_tensor(scfb[:, 0:1024], stage, cc[:, tb:tb + 1024], ALU.mult),
                                   waits=[(sS, ev)] + scfrel, inc=sV)
                        m2 = P.add("vector", lambda e, tb=tb:
                                   e.tensor_tensor(scf[:, 0:1024], PSA[:, SWAP:SWAP + 1024],
                                                   ssn[:, tb:tb + 1024], ALU.mult),
                                   waits=[(sT, swl)], inc=sV)
                        swaprel.append((sV, m2))
                        rp = P.add("vector", lambda e, dst=dst:
                                   e.tensor_tensor(dst, scfb[:, 0:1024], scf[:, 0:1024], ALU.add),
                                   waits=rowait, inc=sV)
                        scfrel[:] = [(sV, rp)]
                        strel[sl] = [(sV, m1), (sT, swl), (sS, sq)]
                        if kind == "q":
                            osem = dO0 if sl == 0 else dO1
                            sv_ = P.add("scalar", lambda e, dst=dst, idx=idx:
                                        e.dma_start(q_d.ap()[idx], dst),
                                        waits=[(sV, rp)], inc=osem, inck=16)
                            rorel[sl] = [(osem, sv_)]
                            qstore.append((osem, sv_))
                    deferred.append(dk)

            for dfn in deferred:
                dfn()
            deferred = []


        phase_A_half(0, do_q=False)
        P.flush()
        phase_A_half(1, do_q=False)
        P.flush()

        def phase_A_q():
            deferred = []
            scfrel = [(sV, sV.n)]
            for hq in range(NQ):
                sl = hq % 2
                ssem = dG0 if sl == 0 else dG1
                wsrc = wq_h.ap()[hq].rearrange("f p c -> p f c")
                wld = P.add("gpsimd", lambda e, wsrc=wsrc, sl=sl:
                            e.dma_start(wbuf[:, sl * WSL:(sl + 1) * WSL]
                                        .rearrange("p (f c) -> p f c", c=128), wsrc),
                            waits=wrel[sl], inc=ssem, inck=16)
                wrel[sl] = []
                pb = MMS[sl]
                pw = [(ssem, wld)] + psrel[sl]
                mml = None
                for fc in range(NF):
                    for (p0, p1) in _pieces(0, 1024):
                        mml = P.add("tensor",
                                    lambda e, fc=fc, p0=p0, p1=p1, pb=pb, sl=sl:
                                    e.matmul(PSA[:, pb + p0:pb + p1], wsl(sl, fc),
                                             y1[:, fc * 1024 + p0:fc * 1024 + p1],
                                             start=(fc == 0), stop=(fc == NF - 1)),
                                    waits=pw if fc == 0 else (),
                                    inc=sT if (fc == NF - 1 and p1 == 1024) else None)
                        pw = ()
                wrel[sl] = [(sT, mml)]
                for dfn in deferred:
                    dfn()
                deferred = []
                stage = sbf[:, 2048 + sl * 1024:2048 + (sl + 1) * 1024]
                ev = P.add("scalar", lambda e, stage=stage, pb=pb: e.copy(stage, PSA[:, pb:pb + 1024]),
                           waits=[(sT, mml)] + strel[sl], inc=sS)
                psrel[sl] = [(sS, ev)]
                sqs = sbf[:, 4096 + sl * 1024:4096 + (sl + 1) * 1024]
                sq = P.add("scalar", lambda e, stage=stage, sqs=sqs:
                           e.activation(sqs, stage, AF.Square),
                           waits=sqrel[sl], inc=sS)
                sqrel[sl] = []

                def dq(stage=stage, sqs=sqs, sq=sq, hq=hq, sl=sl, ev=ev):
                    p0v = P.add("tensor", lambda e, sqs=sqs:
                                e.matmul(PSA[0:1, STATS:STATS + 512], onc[:],
                                         sqs[:, 0:512], start=True, stop=True),
                                waits=[(sS, sq)] + nrmrel[-1:], inc=sT)
                    sww = [(sS, ev)] + swaprel[-1:]
                    swl = None
                    for (p0, p1) in _pieces(0, 1024):
                        swl = P.add("tensor", lambda e, stage=stage, p0=p0, p1=p1:
                                    e.matmul(PSA[:, SWAP + p0:SWAP + p1], psw[:],
                                             stage[:, p0:p1], start=True, stop=True),
                                    waits=sww, inc=sT)
                        sww = ()
                    r0 = P.add("vector", lambda e:
                               e.reduce_max(nt[0:1, 0:1], PSA[0:1, STATS:STATS + 512], axis=AX.X),
                               waits=[(sT, p0v)] + ntrel[-1:], inc=sV)
                    p1v = P.add("tensor", lambda e, sqs=sqs:
                                e.matmul(PSA[0:1, STATS:STATS + 512], onc[:],
                                         sqs[:, 512:1024], start=True, stop=True),
                                waits=[(sV, r0)], inc=sT)
                    r1 = P.add("vector", lambda e:
                               e.reduce_max(nt[0:1, 1:2], PSA[0:1, STATS:STATS + 512], axis=AX.X),
                               waits=[(sT, p1v)], inc=sV)
                    nrmrel.append((sV, r1))
                    mx = P.add("vector", lambda e:
                               e.tensor_tensor(nt[0:1, 2:3], nt[0:1, 0:1], nt[0:1, 1:2], ALU.max),
                               inc=sV)
                    nv = P.add("vector", lambda e, hq=hq:
                               e.tensor_copy(qmax_row[0:1, hq:hq + 1], nt[0:1, 2:3]),
                               waits=[(sV, mx)], inc=sV)
                    ntrel.append((sV, nv))
                    sqrel[sl] = [(sT, p1v)]
                    ro = sbf[:, sl * 1024:(sl + 1) * 1024]
                    m1 = P.add("vector", lambda e, stage=stage:
                               e.tensor_tensor(scfb[:, 0:1024], stage, cc[:, 1024:2048], ALU.mult),
                               waits=[(sS, ev)] + scfrel, inc=sV)
                    m2 = P.add("vector", lambda e:
                               e.tensor_tensor(scf[:, 0:1024], PSA[:, SWAP:SWAP + 1024],
                                               ssn[:, 1024:2048], ALU.mult),
                               waits=[(sT, swl)], inc=sV)
                    swaprel.append((sV, m2))
                    rp = P.add("vector", lambda e, ro=ro:
                               e.tensor_tensor(ro, scfb[:, 0:1024], scf[:, 0:1024], ALU.add),
                               waits=rorel[sl], inc=sV)
                    scfrel[:] = [(sV, rp)]
                    strel[sl] = [(sV, m1), (sT, swl), (sS, sq)]
                    osem = dO0 if sl == 0 else dO1
                    sv_ = P.add("scalar", lambda e, ro=ro, hq=hq: e.dma_start(q_d.ap()[hq], ro),
                                waits=[(sV, rp)], inc=osem, inck=16)
                    rorel[sl] = [(osem, sv_)]
                    qstore.append((osem, sv_))
                deferred.append(dq)
            for dfn in deferred:
                dfn()

        phase_A_q()
        P.flush()

        lq = P.add("scalar", lambda e: e.activation(srow[0:1, 0:32], qmax_row[:], AF.Ln, bias=tinyc[0:1, :]),
                   waits=ntrel[-1:], inc=sS)
        lk = P.add("scalar", lambda e: e.activation(srow[0:1, 32:40], kmax_row[:], AF.Ln, bias=tinyc[0:1, :]),
                   inc=sS)
        sm_last = None
        for j in range(4):
            sm_last = P.add("vector", lambda e, j=j: e.tensor_tensor(
                srow.ap().rearrange("p (a k g) -> p a k g", a=8, k=8, g=4)[:, 2, :, j],
                srow.ap().rearrange("p (a k g) -> p a k g", a=8, k=8, g=4)[:, 0, :, j],
                srow[0:1, 32:40], ALU.add),
                waits=[(sS, lk)] + ([(sV, sm_last)] if sm_last else []), inc=sV)
        ex1 = P.add("scalar", lambda e: e.activation(srow[0:1, 96:128], srow[0:1, 64:96], AF.Exp, scale=0.5),
                    waits=[(sV, sm_last)], inc=sS)
        ngv = P.add("scalar", lambda e: e.activation(srow[0:1, 128:160], srow[0:1, 96:128], AF.Copy, scale=-SM),
                    waits=[(sS, ex1)], inc=sS)
        z4 = P.add("vector", lambda e: e.tensor_tensor(srow[0:1, 160:192], srow[0:1, 128:160], boff[:], ALU.add),
                   waits=[(sS, ngv)] + consts_w, inc=sV)
        P.add("tensor", lambda e: e.matmul(PSA[:, 128:160], onr[:], srow[0:1, 128:160], start=True, stop=True),
              waits=[(sV, z4)] + swaprel[-1:] + nrmrel[-1:], inc=None)
        bb = P.add("tensor", lambda e: e.matmul(PSA[:, 160:192], onr[:], srow[0:1, 160:192], start=True, stop=True),
                   inc=sT)
        btv1 = P.add("scalar", lambda e: e.copy(btabB[:], PSA[:, 128:160]), waits=[(sT, bb)], inc=sS)
        btv2 = P.add("scalar", lambda e: e.copy(btabA[:], PSA[:, 160:192]), inc=sS)
        if DEV_STOP == "A":
            P.flush()
            _dump(nc, P, dG, [("y1", y1), ("k_fm", k_fm), ("v_tm", v_tm),
                              ("btabA", btabA), ("btabB", btabB)])
            return nc
        P.flush()
        sA.close()
        PSC = es.enter_context(nc.psum_tensor("psc", [128, 512], F32))

        sBo = contextlib.ExitStack()
        es.enter_context(sBo)
        attn = sb("attn", [128, NQ * 1024], BF16, sBo)
        wbufC = sb("wbufC", [128, 2 * 4096], BF16, sBo)
        sBi = contextlib.ExitStack()
        es.enter_context(sBi)
        Ebuf = sb("Ebuf", [128, E_TOT], BF16, sBi)
        qsl = sb("qsl", [128, 2 * 1024], BF16, sBi)
        recbc = sb("recbc", [128, 2 * 1024], BF16, sBi)

        AV0 = 2048
        head_pe = {}
        mask_last = {}
        expd = {}
        qld = {}
        mo_prev = {}
        avfree_prev = {}
        rcc_prev = {}
        rec_read_prev = {}

        def b_norm_chain(hq, avv):
            sl = hq % 2
            n0 = P.add("scalar", lambda e: e.activation(rec_row[0:1, 0:512], PSA[0:1, STATS:STATS + 512], AF.Ln),
                       waits=[(sT, avv)], inc=sS)
            n1 = P.add("scalar", lambda e: e.activation(rec_row[0:1, 512:1024], PSC[0:1, 0:512], AF.Ln),
                       inc=sS)
            bw = [] if hq == 0 else [rec_read_prev[hq - 1][1]]
            P.add("scalar", lambda e: e.activation(rec_bf[0:1, 0:512], rec_row[0:1, 0:512],
                                                   AF.Exp, scale=-1.0), waits=bw, inc=sS)
            e1 = P.add("scalar", lambda e: e.activation(rec_bf[0:1, 512:1024], rec_row[0:1, 512:1024],
                                                        AF.Exp, scale=-1.0), inc=sS)
            bsem = dL0 if sl == 0 else dL1
            stv = P.add("sync", lambda e, sl=sl: e.dma_start(rec_d.ap()[sl:sl + 1, :], rec_bf[0:1, :]),
                        waits=[(sS, e1)], inc=bsem, inck=16)
            w = []
            if hq >= 2:
                w.append((sV, mo_prev[hq - 2]))
            bc = P.add("sync", lambda e, sl=sl:
                       e.dma_start(recbc[:, sl * 1024:(sl + 1) * 1024],
                                   rec_d.ap()[sl].partition_broadcast(128)),
                       waits=[(bsem, stv)] + w, inc=bsem, inck=16)
            avcp = P.add("vector", lambda e, hq=hq:
                         e.tensor_copy(attn[:, hq * 1024:(hq + 1) * 1024], PSA[:, AV0:AV0 + 1024]),
                         waits=[(sT, avv)], inc=sV)
            mo = P.add("vector", lambda e, hq=hq, sl=sl:
                       e.tensor_tensor(attn[:, hq * 1024:(hq + 1) * 1024],
                                       attn[:, hq * 1024:(hq + 1) * 1024],
                                       recbc[:, sl * 1024:(sl + 1) * 1024], ALU.mult),
                       waits=[(bsem, bc)], inc=sV)
            mo_prev[hq] = mo
            avfree_prev[hq] = (sV, avcp)
            rcc_prev[hq] = (sS, n1)
            rec_read_prev[hq] = ((sS, e1), (bsem, stv))

        wo_pre = {}
        for oc in (0, 1):
            wsrc = wo_h.ap()[oc].rearrange("f p c -> p f c")
            ssem = dG0 if oc == 0 else dG1
            wo_pre[oc] = P.add("gpsimd", lambda e, wsrc=wsrc, oc=oc:
                               e.dma_start(wbufC[:, oc * 4096:(oc + 1) * 4096]
                                           .rearrange("p (f c) -> p f c", c=128), wsrc),
                               waits=list(qstore), inc=ssem, inck=16)

        B_JOBS = [(0, 1), (2, 3), (4,), (5,), (6,), (7,), (8,), (9,), (10,),
                  (11,), (12, 13), (14, 15)]
        half_guard = {0: 0, 1: 0}
        for hq in range(NQ):
            kh = hq // G
            sl = hq % 2
            qsem = dL0 if sl == 0 else dL1
            w = list(qstore)
            if hq >= 2:
                w.append((sT, head_pe[hq - 2]))
            qld[hq] = P.add("sync", lambda e, hq=hq, sl=sl:
                            e.dma_start(qsl[:, sl * 1024:(sl + 1) * 1024], q_d.ap()[hq]),
                            waits=w, inc=qsem, inck=16)
            mkv = {}
            for ji in range(len(B_JOBS) + 3):
                if ji < len(B_JOBS):
                    job = B_JOBS[ji]
                    half = ji % 2
                    sb0 = half * 1024
                    swt = [(qsem, qld[hq]), (sS, half_guard[half])]
                    if hq == 0 and ji < 2:
                        swt.append((sS, btv2))
                    if hq >= 1 and ji < 2:
                        swt += [(sT, head_pe[hq - 1]), (sV, mask_last[hq - 1])]
                    off = 0
                    scl = None
                    wtot = 0
                    for st in job:
                        qlo, qhi, eo = STRIPS[st]
                        for (p0, p1) in _pieces(off, off + qhi - qlo):
                            scl = P.add("tensor",
                                        lambda e, st=st, p0=p0, p1=p1, sb0=sb0, kh=kh, sl=sl, off=off, qlo=qlo:
                                        e.matmul(PSA[:, sb0 + p0:sb0 + p1],
                                                 k_fm[:, kh * TALL + st * 128:kh * TALL + (st + 1) * 128],
                                                 qsl[:, sl * 1024 + qlo + (p0 - off):sl * 1024 + qlo + (p1 - off)],
                                                 start=True, stop=True),
                                        waits=swt, inc=sT)
                            swt = ()
                        off += qhi - qlo
                    wtot = off
                    bt = btabA if job[0] < 8 else btabB
                    eo0 = STRIPS[job[0]][2]
                    half_guard[half] = P.add(
                        "scalar", lambda e, wtot=wtot, eo0=eo0, sb0=sb0, bt=bt, hq=hq:
                        e.activation(Ebuf[:, eo0:eo0 + wtot], PSA[:, sb0:sb0 + wtot],
                                     AF.Exp, bias=bt[:, hq:hq + 1], scale=SM),
                        waits=[(sT, scl), (sS, btv2)], inc=sS)
                    for st in job:
                        qlo, qhi, eo = STRIPS[st]
                        if st >= 8:
                            mt, mo_ = diag, (st - 8) * 128 - qlo
                        else:
                            mt, mo_ = win, st * 128 - qlo
                        mkv[st] = P.add("vector", lambda e, eo=eo, mo_=mo_, mt=mt:
                                        e.tensor_tensor(Ebuf[:, eo + mo_:eo + mo_ + 128],
                                                        Ebuf[:, eo + mo_:eo + mo_ + 128], mt[:], ALU.mult),
                                        waits=[(sS, half_guard[half])], inc=sV)
                if ji == 1:
                    if hq >= 1:
                        b_norm_chain(hq - 1, head_pe[hq - 1])
                    ow = [rcc_prev[hq - 1]] if hq >= 1 else []
                    P.add("tensor", lambda e: e.matmul(PSA[0:1, STATS:STATS + 512], onc[:], z512[:],
                                                       start=True, stop=False),
                          waits=[sv for sv in ow if sv[1] is not None], inc=None)
                    P.add("tensor", lambda e: e.matmul(PSC[0:1, 0:512], onc[:], z512[:],
                                                       start=True, stop=False), inc=None)
                    ow2 = [avfree_prev[hq - 1]] if hq >= 1 else []
                    P.add("tensor", lambda e: e.matmul(PSA[:, AV0:AV0 + 512], psw[:], z512[:],
                                                       start=True, stop=False),
                          waits=[sv for sv in ow2 if sv[1] is not None], inc=None)
                    P.add("tensor", lambda e: e.matmul(PSA[:, AV0 + 512:AV0 + 1024], psw[:], z512[:],
                                                       start=True, stop=False), inc=None)
                if ji >= 3:
                    for st in B_JOBS[ji - 3]:
                        qlo, qhi, eo = STRIPS[st]
                        dw = [(sV, mkv[st])]
                        for (p0, p1) in _pieces(qlo, qhi):
                            lo = p0 < 512
                            tgt = (PSA[0:1, STATS + p0:STATS + p1] if lo
                                   else PSC[0:1, p0 - 512:p1 - 512])
                            dstop = (st == 11) if lo else (st == 15)
                            P.add("tensor", lambda e, tgt=tgt, p0=p0, p1=p1, eo=eo, qlo=qlo, dstop=dstop:
                                  e.matmul(tgt, onc[:], Ebuf[:, eo + p0 - qlo:eo + p1 - qlo],
                                           start=False, stop=dstop),
                                  waits=dw, inc=None)
                            dw = ()
                        for (p0, p1) in _pieces(qlo, qhi):
                            astop = (st == 11 and p1 == 512) or (st == 15 and p1 == 1024)
                            islast = (st == 15 and p1 == qhi)
                            v_ = P.add("tensor", lambda e, st=st, p0=p0, p1=p1, eo=eo, qlo=qlo, kh=kh, astop=astop:
                                  e.matmul(PSA[:, AV0 + p0:AV0 + p1],
                                           v_tm[:, st * 1024 + kh * 128:st * 1024 + (kh + 1) * 128],
                                           Ebuf[:, eo + p0 - qlo:eo + (p1 - qlo)],
                                           start=False, stop=astop),
                                  waits=(), inc=sT if islast else None)
                            if islast:
                                avv = v_
            mask_last[hq] = mkv[15]
            head_pe[hq] = avv
        b_norm_chain(NQ - 1, head_pe[NQ - 1])
        prev_rec_v, prev_rec_s = rec_read_prev[NQ - 1]
        if DEV_STOP == "B":
            P.flush()
            _dump(nc, P, dG, [("attn", attn)])
            return nc
        sBi.close()

        sC1 = contextlib.ExitStack()
        es.enter_context(sC1)
        astg = sb("astg", [128, 2 * 1024], BF16, sC1)
        asq = sb("asq", [128, 2 * 1024], BF16, sC1)

        wrelC = {0: [], 1: []}
        psrelC = {0: [], 1: []}
        strelC = {0: [], 1: []}
        sqrelC1 = {0: [], 1: []}
        stats_last = None
        for oc in range(NF):
            sl = oc % 2
            ssem = dG0 if sl == 0 else dG1
            if oc < 2:
                wld = wo_pre[oc]
            else:
                wsrc = wo_h.ap()[oc].rearrange("f p c -> p f c")
                wld = P.add("gpsimd", lambda e, wsrc=wsrc, sl=sl:
                            e.dma_start(wbufC[:, sl * 4096:(sl + 1) * 4096].rearrange("p (f c) -> p f c", c=128), wsrc),
                            waits=wrelC[sl], inc=ssem, inck=16)
            wrelC[sl] = []
            pb = MMS[sl]
            pw = [(ssem, wld)] + psrelC[sl]
            if oc == 0:
                pw.append(prev_rec_v)
                pw.append(prev_rec_s)
                pw.append((sS, half_guard[1]))
            if oc == 1:
                pw.append((sV, mo_prev[NQ - 1]))
            mml = None
            for hc in range(NF):
                fw = list(pw) + ([(sV, mo_prev[hc])] if oc < 2 else [])
                for (p0, p1) in _pieces(0, 1024):
                    mml = P.add("tensor", lambda e, hc=hc, p0=p0, p1=p1, pb=pb, sl=sl:
                                e.matmul(PSA[:, pb + p0:pb + p1],
                                         wbufC[:, sl * 4096 + hc * 128:sl * 4096 + (hc + 1) * 128],
                                         attn[:, hc * 1024 + p0:hc * 1024 + p1],
                                         start=(hc == 0), stop=(hc == NF - 1)),
                                waits=fw,
                                inc=sT if (hc == NF - 1 and p1 == 1024) else None)
                    fw = ()
                    pw = ()
            wrelC[sl] = [(sT, mml)]
            stg = astg[:, sl * 1024:(sl + 1) * 1024]
            ev = P.add("scalar", lambda e, stg=stg, pb=pb: e.copy(stg, PSA[:, pb:pb + 1024]),
                       waits=[(sT, mml)] + strelC[sl], inc=sS)
            psrelC[sl] = [(sS, ev)]
            sqs = asq[:, sl * 1024:(sl + 1) * 1024]
            sq = P.add("scalar", lambda e, stg=stg, sqs=sqs: e.activation(sqs, stg, AF.Square),
                       waits=sqrelC1[sl], inc=sS)
            sqrelC1[sl] = []
            stl = None
            for pi, (p0, p1) in enumerate(_pieces(0, 1024)):
                sumtgt = PSA[0:1, p0:p1]
                sqtgt = PSA[0:1, STATS:STATS + 512] if pi == 0 else PSC[0:1, 0:512]
                P.add("tensor", lambda e, stg=stg, p0=p0, p1=p1, sumtgt=sumtgt, oc=oc:
                      e.matmul(sumtgt, onc[:], stg[:, p0:p1],
                               start=(oc == 0), stop=(oc == NF - 1)),
                      waits=[(sS, ev)], inc=None)
                stl = P.add("tensor", lambda e, sqs=sqs, p0=p0, p1=p1, sqtgt=sqtgt, oc=oc:
                            e.matmul(sqtgt, onc[:], sqs[:, p0:p1],
                                     start=(oc == 0), stop=(oc == NF - 1)),
                            waits=[(sS, sq)], inc=sT)
            stats_last = stl
            sqrelC1[sl] = [(sT, stl)]
            osem = dO0 if sl == 0 else dO1
            sv_ = P.add("scalar", lambda e, stg=stg, oc=oc: e.dma_start(a_d.ap()[oc], stg),
                        waits=[(sS, sq)], inc=osem, inck=16)
            strelC[sl] = [(osem, sv_), (sT, stl), (sS, sq)]
        a_store = [(dO0, dO0.n), (dO1, dO1.n)]
        P.flush()
        sBi.close()
        sC1.close()
        sBo.close()
        s1.close()

        sC = contextlib.ExitStack()
        es.enter_context(sC)
        slab0 = sb("slab0", [128, NF * 1024], BF16, sC)
        wbufD = sb("wbufD", [128, 2 * 4096], BF16, sC)
        xfE = None

        def ln_rows(stats_wait, extra=()):
            c1 = P.add("scalar", lambda e: e.activation(m_row[:, 0:1024], PSA[0:1, 0:1024],
                                                        AF.Copy, scale=1.0 / HID),
                       waits=[(sT, stats_wait)] + list(extra), inc=sS)
            c3 = P.add("scalar", lambda e: e.activation(v_row[:, 0:512], PSA[0:1, STATS:STATS + 512],
                                                        AF.Copy, scale=1.0 / HID), inc=sS)
            c4 = P.add("scalar", lambda e: e.activation(v_row[:, 512:1024], PSC[0:1, 0:512],
                                                        AF.Copy, scale=1.0 / HID), inc=sS)
            t1 = P.add("vector", lambda e: e.tensor_tensor(mr_row[:], m_row[:], m_row[:], ALU.mult),
                       waits=[(sS, c4)], inc=sV)
            t2 = P.add("vector", lambda e: e.tensor_tensor(v_row[:], v_row[:], mr_row[:], ALU.subtract),
                       waits=[(sV, t1)], inc=sV)
            l1 = P.add("scalar", lambda e: e.activation(r_row[:], v_row[:], AF.Ln, bias=epsc[0:1, :]),
                       waits=[(sV, t2)], inc=sS)
            l2 = P.add("scalar", lambda e: e.activation(r_row[:], r_row[:], AF.Exp, scale=-0.5),
                       waits=[(sS, l1)], inc=sS)
            t3 = P.add("vector", lambda e: e.tensor_tensor(mr_row[:], m_row[:], r_row[:], ALU.mult),
                       waits=[(sS, l2)], inc=sV)
            bcv = None
            for (p0, p1) in _pieces(0, 1024):
                P.add("tensor", lambda e, p0=p0, p1=p1:
                      e.matmul(PSA[:, 1024 + p0:1024 + p1], onr[:], r_row[:, p0:p1],
                               start=True, stop=True),
                      waits=[(sS, l2)], inc=None)
                bcv = P.add("tensor", lambda e, p0=p0, p1=p1:
                      e.matmul(PSA[:, 2048 + p0:2048 + p1], onr[:], mr_row[:, p0:p1],
                               start=True, stop=True),
                      waits=[(sV, t3)], inc=sT)
            g1 = P.add("scalar", lambda e: e.copy(grR[:], PSA[:, 1024:2048]),
                       waits=[(sT, bcv)], inc=sS)
            g2 = P.add("scalar", lambda e: e.copy(grMR[:], PSA[:, 2048:3072]), inc=sS)
            return g2

        gC = ln_rows(stats_last)

        sC3 = contextlib.ExitStack()
        es.enter_context(sC3)
        NSL = 6
        ach = sb("ach", [128, NSL * 1024], BF16, sC3)
        scCb = sb("scCb", [128, NSL * 1024], BF16, sC3)
        xfs = sb("xfs", [128, NSL * 1024], F32, sC3)
        sqC = sb("sqC", [128, 2 * 1024], BF16, sC3)

        arelC = {i: [] for i in range(NSL)}
        xrelC = {i: [] for i in range(NSL)}
        sqrelC3 = {0: [], 1: []}
        st2_cell = [None]
        frontC = {}

        def c3_front(oc):
            sl = oc % NSL
            asem = dL0 if oc % 2 == 0 else dL1
            xsem = dT0 if oc % 2 == 0 else dT1
            av = P.add("sync", lambda e, oc=oc, sl=sl:
                       e.dma_start(ach[:, sl * 1024:(sl + 1) * 1024], a_d.ap()[oc]),
                       waits=arelC[sl] + (a_store if oc < 2 else []), inc=asem, inck=16)
            xv = P.add("scalar", lambda e, oc=oc, sl=sl:
                       e.dma_start(xfs[:, sl * 1024:(sl + 1) * 1024],
                                   x_h.ap()[oc * 128:(oc + 1) * 128, 1024:2048]),
                       waits=xrelC[sl], inc=xsem, inck=16)
            sb_ = scCb[:, sl * 1024:(sl + 1) * 1024]
            ee = "vector"
            esem = sV
            d1 = P.add(ee, lambda e, sb_=sb_, sl=sl:
                       e.tensor_tensor(sb_, ach[:, sl * 1024:(sl + 1) * 1024], grR[:], ALU.mult),
                       waits=[(asem, av), (sS, gC)], inc=esem)
            P.add(ee, lambda e, sb_=sb_: e.tensor_tensor(sb_, sb_, grMR[:], ALU.subtract), inc=esem)
            arelC[sl] = [(esem, d1)]
            frontC[oc] = (sl, sb_, ee, esem, xsem, xv)

        def c3_back(oc):
            sl, sb_, ee, esem, xsem, xv = frontC[oc]
            s0 = slab0[:, oc * 1024:(oc + 1) * 1024]
            d3 = P.add(ee, lambda e, s0=s0, sb_=sb_, sl=sl:
                       e.tensor_tensor(s0, sb_, xfs[:, sl * 1024:(sl + 1) * 1024], ALU.add),
                       waits=[(xsem, xv)], inc=esem)
            xrelC[sl] = [(esem, d3)]
            ssl = oc % 2
            osem = dG0 if ssl == 0 else dG1
            P.add("gpsimd", lambda e, s0=s0, oc=oc: e.dma_start(x1_d.ap()[oc], s0),
                  waits=[(esem, d3)], inc=osem, inck=16)
            sqs = sqC[:, ssl * 1024:(ssl + 1) * 1024]
            sq = P.add("scalar", lambda e, sqs=sqs, s0=s0:
                       e.activation(sqs, s0, AF.Square),
                       waits=sqrelC3[ssl] + [(esem, d3)], inc=sS)
            sqrelC3[ssl] = []
            stl = None
            for pi, (p0, p1) in enumerate(_pieces(0, 1024)):
                sumtgt = PSA[0:1, p0:p1]
                sqtgt = PSA[0:1, STATS:STATS + 512] if pi == 0 else PSC[0:1, 0:512]
                P.add("tensor", lambda e, oc=oc, p0=p0, p1=p1, sumtgt=sumtgt:
                      e.matmul(sumtgt, onc[:],
                               slab0[:, oc * 1024 + p0:oc * 1024 + p1],
                               start=(oc == 0), stop=(oc == NF - 1)),
                      waits=[(esem, d3)], inc=None)
                stl = P.add("tensor", lambda e, sqs=sqs, p0=p0, p1=p1, sqtgt=sqtgt, oc=oc:
                            e.matmul(sqtgt, onc[:], sqs[:, p0:p1],
                                     start=(oc == 0), stop=(oc == NF - 1)),
                            waits=[(sS, sq)], inc=sT)
            st2_cell[0] = stl
            sqrelC3[ssl] = [(sT, stl)]

        u1_pre = {}
        for r in (0, 1):
            wsrc = u1_h.ap()[r].rearrange("f p c -> p f c")
            ssem = dG0 if r == 0 else dG1
            u1_pre[r] = P.add("gpsimd", lambda e, wsrc=wsrc, r=r:
                              e.dma_start(wbufD[:, r * 4096:(r + 1) * 4096]
                                          .rearrange("p (f c) -> p f c", c=128), wsrc),
                              waits=list(a_store), inc=ssem, inck=16)
        SKEW = 2
        for oc in range(NF):
            c3_front(oc)
            if oc >= SKEW:
                c3_back(oc - SKEW)
        for oc in range(NF - SKEW, NF):
            c3_back(oc)
        st2_last = st2_cell[0]
        x1_store = [(dG0, dG0.n), (dG1, dG1.n)]
        for _eng in ("tensor", "scalar", "vector", "gpsimd"):
            P.add(_eng, lambda e: e.nop(), waits=x1_store)
        P.flush()
        sC3.close()

        gC2 = ln_rows(st2_last)
        yvs = {}
        sCy = contextlib.ExitStack()
        es.enter_context(sCy)
        scY = sb("scY", [128, 4 * 1024], BF16, sCy)
        yrel = {0: [], 1: [], 2: [], 3: []}
        for oc in range(NF):
            sl = oc % 4
            s0 = scY[:, sl * 1024:(sl + 1) * 1024]
            ee = "vector"
            esem = sV
            P.add(ee, lambda e, s0=s0, oc=oc:
                  e.tensor_tensor(s0, slab0[:, oc * 1024:(oc + 1) * 1024], grR[:], ALU.mult),
                  waits=[(sS, gC2)] + yrel[sl], inc=esem)
            yv = P.add(ee, lambda e, s0=s0, oc=oc:
                       e.tensor_tensor(slab0[:, oc * 1024:(oc + 1) * 1024], s0, grMR[:], ALU.subtract),
                       inc=esem)
            yrel[sl] = [(esem, yv)]
            yvs[oc] = (esem, yv)
        sCy.close()

        if DEV_STOP == "C":
            _dump(nc, P, dG, [("slab0", slab0)])
            return nc

        sD = contextlib.ExitStack()
        es.enter_context(sD)
        Zsb = sb("Zsb", [128, NR * 1024], F32, sD)
        Zbf = sb("Zbf", [128, NR * 1024], BF16, sD)
        Pbf = sb("Pbf", [128, NR * 1024], BF16, sD)
        wu2 = sb("wu2", [128, 2 * (GSZ * 128)], BF16, sD)
        wv1 = wbufD

        sD1 = contextlib.ExitStack()
        es.enter_context(sD1)
        wrelD = {0: [], 1: []}
        psrelD = {0: [], 1: []}
        for r in range(NR):
            sl = r % 2
            ssem = dG0 if sl == 0 else dG1
            if r < 2:
                wld = u1_pre[r]
            else:
                wsrc = u1_h.ap()[r].rearrange("f p c -> p f c")
                wld = P.add("gpsimd", lambda e, wsrc=wsrc, sl=sl:
                            e.dma_start(wbufD[:, sl * 4096:(sl + 1) * 4096].rearrange("p (f c) -> p f c", c=128), wsrc),
                            waits=wrelD[sl], inc=ssem, inck=16)
            wrelD[sl] = []
            pb = MMS[sl]
            pw = [(ssem, wld)] + psrelD[sl]
            mml = None
            for fc in range(NF):
                fw = list(pw) + ([yvs[fc]] if r < 2 else [])
                for (p0, p1) in _pieces(0, 1024):
                    mml = P.add("tensor", lambda e, fc=fc, p0=p0, p1=p1, pb=pb, sl=sl:
                                e.matmul(PSA[:, pb + p0:pb + p1],
                                         wbufD[:, sl * 4096 + fc * 128:sl * 4096 + (fc + 1) * 128],
                                         slab0[:, fc * 1024 + p0:fc * 1024 + p1],
                                         start=(fc == 0), stop=(fc == NF - 1)),
                                waits=fw,
                                inc=sT if (fc == NF - 1 and p1 == 1024) else None)
                    fw = ()
                    pw = ()
            wrelD[sl] = [(sT, mml)]
            ev = P.add("scalar", lambda e, r=r, pb=pb:
                       e.copy(Pbf[:, r * 1024:(r + 1) * 1024], PSA[:, pb:pb + 1024]),
                       waits=[(sT, mml)], inc=sS)
            psrelD[sl] = [(sS, ev)]
        pbf_done = sS.n
        v1_pre = {}
        for g in (0, 1):
            wsrc = v1_h.ap()[g].rearrange("a r p c -> p a r c")
            ssem = dG0 if g == 0 else dG1
            v1_pre[g] = P.add("gpsimd", lambda e, wsrc=wsrc, g=g:
                              e.dma_start(wv1[:, g * 2048:(g + 1) * 2048]
                                          .rearrange("p (a r c) -> p a r c", a=2, c=128), wsrc),
                              waits=wrelD[0] + wrelD[1], inc=ssem, inck=16)
        sD1.close()

        sD2 = contextlib.ExitStack()
        es.enter_context(sD2)
        actb = sb("actb", [128, GSZ * 1024], BF16, sD2)
        sgst = sb("sgst", [128, 2 * 512], BF16, sD2)
        ubst = sb("ubst", [128, 2 * 512], BF16, sD2)

        HS = [1024, 1536, 2048, 2560]
        wrel1 = {0: [], 1: []}
        wrel2 = {0: [], 1: []}
        hrel = {}
        sgrel = {0: [], 1: []}
        ubrel = {0: [], 1: []}
        zrel = {0: [], 1: []}
        zver = {}
        zfin = {}
        z_prev_group = [0]
        u2_off = [0]
        groups = _ffn_groups()
        NGRP = len(groups)

        def make_z(gi, g0, g1, actdv):
            gsz = g1 - g0

            def emit_z():
                zlast = None
                for r in range(NR):
                    sl2 = r % 2
                    ssem = dG0 if sl2 == 0 else dG1
                    srcz = u2_h.ap()[u2_off[0]:u2_off[0] + gsz].rearrange("i p c -> p i c")
                    u2_off[0] += gsz
                    wld = P.add("gpsimd", lambda e, srcz=srcz, sl2=sl2, gsz=gsz:
                                e.dma_start(wu2[:, sl2 * GSZ * 128:sl2 * GSZ * 128 + gsz * 128]
                                            .rearrange("p (i c) -> p i c", c=128), srcz),
                                waits=wrel2[sl2], inc=ssem, inck=16)
                    wrel2[sl2] = []
                    lastz = None
                    for tp in range(2):
                        zsl = (r * 2 + tp) % 2
                        pw = [(ssem, wld)] + zrel[zsl]
                        for gg in range(gsz):
                            pw2 = list(pw) + [(sV, actdv[(gg, tp)])]
                            lastz = P.add(
                                "tensor", lambda e, gg=gg, tp=tp, zsl=zsl, sl2=sl2, gsz=gsz:
                                e.matmul(PSA[:, zsl * 512:(zsl + 1) * 512],
                                         wu2[:, sl2 * GSZ * 128 + gg * 128:sl2 * GSZ * 128 + (gg + 1) * 128],
                                         actb[:, gg * 1024 + tp * 512:gg * 1024 + (tp + 1) * 512],
                                         start=(gg == 0), stop=(gg == gsz - 1)),
                                waits=pw2, inc=sT if gg == gsz - 1 else None)
                            pw = ()
                        if gi == 0:
                            zv = P.add("vector", lambda e, r=r, tp=tp, zsl=zsl:
                                       e.tensor_copy(Zsb[:, r * 1024 + tp * 512:r * 1024 + (tp + 1) * 512],
                                                     PSA[:, zsl * 512:(zsl + 1) * 512]),
                                       waits=[(sT, lastz)], inc=sV)
                        elif gi == NGRP - 1:
                            zv = P.add("vector", lambda e, r=r, tp=tp, zsl=zsl:
                                       e.tensor_tensor(Zbf[:, r * 1024 + tp * 512:r * 1024 + (tp + 1) * 512],
                                                       Zsb[:, r * 1024 + tp * 512:r * 1024 + (tp + 1) * 512],
                                                       PSA[:, zsl * 512:(zsl + 1) * 512], ALU.add),
                                       waits=[(sT, lastz), (sV, zver.get((r, tp), 0))], inc=sV)
                            zfin[(r, tp)] = zv
                        else:
                            zv = P.add("vector", lambda e, r=r, tp=tp, zsl=zsl:
                                       e.tensor_tensor(Zsb[:, r * 1024 + tp * 512:r * 1024 + (tp + 1) * 512],
                                                       Zsb[:, r * 1024 + tp * 512:r * 1024 + (tp + 1) * 512],
                                                       PSA[:, zsl * 512:(zsl + 1) * 512], ALU.add),
                                       waits=[(sT, lastz), (sV, zver.get((r, tp), 0))], inc=sV)
                        zrel[zsl] = [(sV, zv)]
                        zver[(r, tp)] = zv
                    wrel2[sl2] = [(sT, lastz)]
                z_prev_group[0] = lastz
            return emit_z

        pending_z = None
        for gi, (g0, g1) in enumerate(groups):
            actdv = {}
            for g in range(g0, g1):
                sl = g % 2
                ssem = dG0 if sl == 0 else dG1
                if g < 2:
                    wld = v1_pre[g]
                else:
                    wsrc = v1_h.ap()[g].rearrange("a r p c -> p a r c")
                    wld = P.add("gpsimd", lambda e, wsrc=wsrc, sl=sl:
                                e.dma_start(wv1[:, sl * 2048:(sl + 1) * 2048].rearrange("p (a r c) -> p a r c", a=2, c=128), wsrc),
                                waits=wrel1[sl], inc=ssem, inck=16)
                wrel1[sl] = []
                mml = {}
                for a in range(2):
                    pw = ([(ssem, wld), (sS, pbf_done)] + hrel.get(HS[a * 2], [])
                          + hrel.get(HS[a * 2 + 1], []))
                    for r in range(NR):
                        for tp in range(2):
                            hs = HS[a * 2 + tp]
                            mml[(a, tp)] = P.add(
                                "tensor", lambda e, a=a, r=r, tp=tp, hs=hs, sl=sl:
                                e.matmul(PSA[:, hs:hs + 512],
                                         wv1[:, sl * 2048 + (a * NR + r) * 128:
                                             sl * 2048 + (a * NR + r + 1) * 128],
                                         Pbf[:, r * 1024 + tp * 512:r * 1024 + (tp + 1) * 512],
                                         start=(r == 0), stop=(r == NR - 1)),
                                waits=pw,
                                inc=sT if r == NR - 1 else None)
                            pw = ()
                wrel1[sl] = [(sT, mml[(1, 1)])]
                if g == g0 and pending_z is not None:
                    pending_z()
                    pending_z = None
                sgs = {}
                ubs = {}
                for tp in range(2):
                    ssl = (g * 2 + tp) % 2
                    sgs[tp] = P.add("scalar", lambda e, g=g, tp=tp, ssl=ssl:
                                    e.activation(sgst[:, ssl * 512:(ssl + 1) * 512],
                                                 PSA[:, HS[tp]:HS[tp] + 512], AF.Silu,
                                                 bias=b1t[:, g:g + 1]),
                                    waits=[(sT, mml[(0, tp)])] + sgrel[ssl], inc=sS)
                    hrel[HS[tp]] = [(sS, sgs[tp])]
                for tp in range(2):
                    ssl = (g * 2 + tp) % 2
                    ubs[tp] = P.add("scalar", lambda e, g=g, tp=tp, ssl=ssl:
                                    e.activation(ubst[:, ssl * 512:(ssl + 1) * 512],
                                                 PSA[:, HS[2 + tp]:HS[2 + tp] + 512], AF.Identity,
                                                 bias=b1t[:, NI + g:NI + g + 1]),
                                    waits=[(sT, mml[(1, tp)])] + ubrel[ssl], inc=sS)
                    hrel[HS[2 + tp]] = [(sS, ubs[tp])]
                for tp in range(2):
                    ssl = (g * 2 + tp) % 2
                    dv = P.add("vector", lambda e, g=g, g0=g0, tp=tp, ssl=ssl:
                               e.tensor_tensor(
                                   actb[:, (g - g0) * 1024 + tp * 512:(g - g0) * 1024 + (tp + 1) * 512],
                                   sgst[:, ssl * 512:(ssl + 1) * 512],
                                   ubst[:, ssl * 512:(ssl + 1) * 512], ALU.mult),
                               waits=[(sS, sgs[tp]), (sS, ubs[tp]), (sT, z_prev_group[0])], inc=sV)
                    sgrel[ssl] = [(sV, dv)]
                    ubrel[ssl] = [(sV, dv)]
                    actdv[(g - g0, tp)] = dv
            pending_z = make_z(gi, g0, g1, actdv)
        pending_z()
        v2_pre = {}
        for oc in range(4):
            wsrc = v2_h.ap()[oc].rearrange("r p c -> p r c")
            ssem = dG0 if oc % 2 == 0 else dG1
            v2_pre[oc] = P.add("gpsimd", lambda e, wsrc=wsrc, oc=oc:
                               e.dma_start(wbufD[:, 4096 + oc * 1024:4096 + (oc + 1) * 1024]
                                           .rearrange("p (r c) -> p r c", c=128), wsrc),
                               waits=wrelD[0] + wrelD[1], inc=ssem, inck=16)
        sD2.close()

        xfE = Pbf
        sD3 = contextlib.ExitStack()
        es.enter_context(sD3)
        sqD = sb("sqD", [128, 2 * 1024], BF16, sD3)
        wrel3 = {i: [] for i in range(4)}
        psrel3 = {0: [], 1: []}
        sqrel3 = {0: [], 1: []}
        st3_cell = [None]

        def d3_mm(oc):
            sl4 = oc % 4
            sl = oc % 2
            ssem = dG0 if sl == 0 else dG1
            if oc < 4:
                wld = v2_pre[oc]
            else:
                wsrc = v2_h.ap()[oc].rearrange("r p c -> p r c")
                wld = P.add("gpsimd", lambda e, wsrc=wsrc, sl4=sl4:
                            e.dma_start(wbufD[:, 4096 + sl4 * 1024:4096 + (sl4 + 1) * 1024]
                                        .rearrange("p (r c) -> p r c", c=128), wsrc),
                            waits=wrel3[sl4], inc=ssem, inck=16)
            wrel3[sl4] = []
            pb = MMS[sl]
            pw = [(ssem, wld)] + psrel3[sl]
            mml = None
            for r in range(NR):
                rw = list(pw) + ([(sV, zfin[(r, 0)]), (sV, zfin[(r, 1)])] if oc == 0 else [])
                for (p0, p1) in _pieces(0, 1024):
                    mml = P.add("tensor", lambda e, r=r, p0=p0, p1=p1, pb=pb, sl4=sl4:
                                e.matmul(PSA[:, pb + p0:pb + p1],
                                         wbufD[:, 4096 + sl4 * 1024 + r * 128:4096 + sl4 * 1024 + (r + 1) * 128],
                                         Zbf[:, r * 1024 + p0:r * 1024 + p1],
                                         start=(r == 0), stop=(r == NR - 1)),
                                waits=rw,
                                inc=sT if (r == NR - 1 and p1 == 1024) else None)
                    rw = ()
                pw = ()
            wrel3[sl4] = [(sT, mml)]
            return mml

        def d3_post(oc, mml):
            sl = oc % 2
            pb = MMS[sl]
            ev = P.add("scalar", lambda e, oc=oc, pb=pb:
                       e.activation(slab0[:, oc * 1024:(oc + 1) * 1024], PSA[:, pb:pb + 1024],
                                    AF.Identity, bias=b2t[:, oc:oc + 1]),
                       waits=[(sT, mml)], inc=sS)
            psrel3[sl] = [(sS, ev)]
            sqs = sqD[:, sl * 1024:(sl + 1) * 1024]
            sq = P.add("scalar", lambda e, sqs=sqs, oc=oc:
                       e.activation(sqs, slab0[:, oc * 1024:(oc + 1) * 1024], AF.Square),
                       waits=sqrel3[sl], inc=sS)
            sqrel3[sl] = []
            stl = None
            for pi, (p0, p1) in enumerate(_pieces(0, 1024)):
                sumtgt = PSA[0:1, p0:p1]
                sqtgt = PSA[0:1, STATS:STATS + 512] if pi == 0 else PSC[0:1, 0:512]
                P.add("tensor", lambda e, oc=oc, p0=p0, p1=p1, sumtgt=sumtgt:
                      e.matmul(sumtgt, onc[:],
                               slab0[:, oc * 1024 + p0:oc * 1024 + p1],
                               start=(oc == 0), stop=(oc == NF - 1)),
                      waits=[(sS, ev)] + ([(sS, gC2)] if oc == 0 else []), inc=None)
                stl = P.add("tensor", lambda e, sqs=sqs, p0=p0, p1=p1, sqtgt=sqtgt, oc=oc:
                            e.matmul(sqtgt, onc[:], sqs[:, p0:p1],
                                     start=(oc == 0), stop=(oc == NF - 1)),
                            waits=[(sS, sq)], inc=sT)
            st3_cell[0] = stl
            sqrel3[sl] = [(sT, stl)]

        prevd3 = None
        for oc in range(NF):
            mm = d3_mm(oc)
            if prevd3 is not None:
                d3_post(*prevd3)
            prevd3 = (oc, mm)
        d3_post(*prevd3)
        st3_last = st3_cell[0]
        x1_pre = {}
        for oc in range(4):
            asem = dL0 if oc % 2 == 0 else dL1
            x1_pre[oc] = P.add("sync" if oc % 2 == 0 else "scalar",
                               lambda e, oc=oc:
                               e.dma_start(xfE[:, oc * 1024:(oc + 1) * 1024], x1_d.ap()[oc]),
                               waits=list(x1_store) + wrel1[0] + wrel1[1],
                               inc=asem, inck=16)
        P.flush()
        sD3.close()
        sD.close()

        if DEV_STOP == "D":
            _dump(nc, P, dG, [("slab0", slab0)])
            return nc

        gE = ln_rows(st3_last)
        sE = contextlib.ExitStack()
        es.enter_context(sE)
        scE = sb("scE", [128, 4 * 1024], F32, sE)
        scEb = sb("scEb", [128, 4 * 1024], BF16, sE)
        orel = {0: [], 1: [], 2: [], 3: []}
        xrelE = {0: [], 1: [], 2: [], 3: []}

        def e_front(oc):
            sl4 = oc % 4
            asem = dL0 if oc % 2 == 0 else dL1
            if oc < 4:
                xv = x1_pre[oc]
            else:
                xv = P.add("sync" if oc % 2 == 0 else "scalar",
                           lambda e, oc=oc, sl4=sl4:
                           e.dma_start(xfE[:, sl4 * 1024:(sl4 + 1) * 1024], x1_d.ap()[oc]),
                           waits=xrelE[sl4], inc=asem, inck=16)
            sb_ = scEb[:, sl4 * 1024:(sl4 + 1) * 1024]
            P.add("vector", lambda e, sb_=sb_, oc=oc:
                  e.tensor_tensor(sb_, slab0[:, oc * 1024:(oc + 1) * 1024], grR[:], ALU.mult),
                  waits=[(sS, gE)], inc=sV)
            P.add("vector", lambda e, sb_=sb_: e.tensor_tensor(sb_, sb_, grMR[:], ALU.subtract), inc=sV)
            return (oc, sl4, sb_, xv, asem)

        def e_back(st):
            oc, sl4, sb_, xv, asem = st
            s0 = scE[:, sl4 * 1024:(sl4 + 1) * 1024]
            d3 = P.add("vector", lambda e, s0=s0, sb_=sb_, sl4=sl4:
                       e.tensor_tensor(s0, sb_, xfE[:, sl4 * 1024:(sl4 + 1) * 1024], ALU.add),
                       waits=[(asem, xv)] + orel[sl4], inc=sV)
            xrelE[sl4] = [(sV, d3)]
            osem = dG0 if oc % 2 == 0 else dG1
            sv_ = P.add("gpsimd",
                        lambda e, s0=s0, oc=oc:
                        e.dma_start(out_h.ap()[oc * 128:(oc + 1) * 128, :], s0),
                        waits=[(sV, d3)], inc=osem, inck=16)
            orel[sl4] = [(osem, sv_)]

        prevE = None
        for oc in range(NF):
            st = e_front(oc)
            if prevE is not None:
                e_back(prevE)
            prevE = st
        e_back(prevE)
        P.add("sync", lambda e: e.nop(), waits=[(dG0, dG0.n), (dG1, dG1.n)])
        P.flush()
        sE.close()
        sC.close()

    return nc


def _dump(nc, P, dG, tensors):
    for name, t in tensors:
        h = nc.dram_tensor("dump_" + name, [int(s) for s in t.shape], F32,
                           kind="ExternalOutput")
        P.add("gpsimd", lambda e, h=h, t=t: e.dma_start(h.ap(), t[:]), inc=dG, inck=16)
    P.add("sync", lambda e: e.nop(), waits=[(s, s.n) for s in P.dma_sems])
    P.flush()


def _host_shared(inputs):
    wq = np.asarray(inputs["L0_attn_wq"], np.float32)
    wk = np.asarray(inputs["L0_attn_wk"], np.float32)
    wv = np.asarray(inputs["L0_attn_wv"], np.float32)
    wo = np.asarray(inputs["L0_attn_wo"], np.float32)
    U1 = np.asarray(inputs["L0_ffn_U1"], np.float32)
    V1 = np.asarray(inputs["L0_ffn_V1"], np.float32)
    U2 = np.asarray(inputs["L0_ffn_U2"], np.float32)
    V2 = np.asarray(inputs["L0_ffn_V2"], np.float32)
    b1 = np.asarray(inputs["L0_ffn_b1"], np.float32)
    b2 = np.asarray(inputs["L0_ffn_b2"], np.float32)

    d = {}
    d["wq_b"] = _blk(wq, NF, NQ)
    d["wk_b"] = _blk(wk, NF, NKV)
    d["wv_b"] = _blk(wv, NF, NKV)
    d["wo_b"] = _blk(wo, NF, NF)
    d["u1_b"] = _blk(U1, NF, NR)
    v1b = _blk(V1, NR, NH2)
    d["v1_b"] = np.ascontiguousarray(np.stack([v1b[:NI], v1b[NI:]], axis=1))
    u2b = _blk(U2, NI, NR)
    order = []
    for (i0, i1) in _ffn_groups():
        for r in range(NR):
            for i in range(i0, i1):
                order.append(u2b[r, i])
    d["u2_b"] = np.ascontiguousarray(np.stack(order))
    d["v2_b"] = _blk(V2, NR, NF)
    d["b1_t"] = np.ascontiguousarray(b1.reshape(NH2, 128).T)
    d["b2_t"] = np.ascontiguousarray(b2.reshape(NF, 128).T)
    ii = np.arange(128)
    bf16 = ml_dtypes.bfloat16
    d["diag_m"] = (ii[:, None] <= ii[None, :]).astype(bf16)
    d["win_m"] = (ii[:, None] > ii[None, :]).astype(bf16)
    ps = np.zeros((128, 128), np.float32)
    ps[np.arange(64) + 64, np.arange(64)] = 1.0
    ps[np.arange(64), np.arange(64) + 64] = 1.0
    d["pswap"] = ps.astype(bf16)
    d["ident"] = np.eye(128, dtype=bf16)
    d["ones_row"] = np.ones((1, 128), np.float32)
    d["ones_col"] = np.ones((128, 1), bf16)
    return d


def _host_core(inputs, b, h):
    x = np.asarray(inputs["L0_attn_x"], np.float32)[b]
    own = slice(h * TOWN, (h + 1) * TOWN)
    oth = slice((1 - h) * TOWN, (2 - h) * TOWN)
    xT = x.T
    d = {}
    d["x_rot"] = np.ascontiguousarray(np.concatenate([xT[:, oth], xT[:, own]], axis=1))
    pos = np.arange(TALL)
    pos_rot = np.concatenate([pos[oth], pos[own]]).astype(np.float32)
    inv = THETA ** (-np.arange(64, dtype=np.float32) / 64.0)
    ang = inv[:, None] * pos_rot[None, :]
    d["cc_t"] = np.ascontiguousarray(
        np.concatenate([np.cos(ang), np.cos(ang)], axis=0)).astype(ml_dtypes.bfloat16)
    d["ss_t"] = np.ascontiguousarray(
        np.concatenate([-np.sin(ang), np.sin(ang)], axis=0)).astype(ml_dtypes.bfloat16)
    d["bias_off"] = np.full((1, 32), 0.0 if h == 1 else NEG, np.float32)
    return d


def kernel(**inputs):
    shared = _host_shared(inputs)
    cores = []
    pairs = [(b, h) for b in range(4) for h in range(2)]
    for b, h in pairs:
        m = dict(shared)
        m.update(_host_core(inputs, b, h))
        cores.append(m)
    nc = build_program()
    res = run_bass_kernel_spmd(nc, cores, core_ids=list(range(8)))
    out = np.empty((4, TALL, HID), np.float32)
    for i, (b, h) in enumerate(pairs):
        out[b, h * TOWN:(h + 1) * TOWN, :] = np.asarray(res.results[i]["out"]).T
    return out


# version 22
# speedup vs baseline: 1.0497x; 1.0497x over previous
import os
import contextlib
import numpy as np
import ml_dtypes
import concourse.bass as bass
import concourse.mybir as mybir
from concourse import dve_ops
from concourse.bass_utils import run_bass_kernel_spmd

F32 = mybir.dt.float32
BF16 = mybir.dt.bfloat16
AF = mybir.ActivationFunctionType
ALU = mybir.AluOpType
AX = mybir.AxisListType

HID = 4096
NF = 32
D = 128
NQ = 32
NKV = 8
G = 4
TOWN = 1024
TALL = 2048
SM = float(D) ** -0.5
EPS = 1e-5
NR = 8
INTER = 11008
NI = 86
NH2 = 172
GSZ = 8
NEG = -1e30
THETA = 10000.0

DEV_STOP = os.environ.get("MKDEV_STOP", "")

STRIPS = []
_eo = 0
for _st in range(16):
    _qlo = max(0, (_st - 8) * 128)
    _qhi = min(TOWN, (_st + 1) * 128)
    STRIPS.append((_qlo, _qhi, _eo))
    _eo += _qhi - _qlo
E_TOT = _eo


def _pieces(lo, hi, step=512):
    out, p = [], lo
    while p < hi:
        p2 = min(hi, (p // step + 1) * step)
        out.append((p, p2))
        p = p2
    return out


def _ffn_groups():
    g, i = [], 0
    while i < NI:
        g.append((i, min(NI, i + GSZ)))
        i += GSZ
    return g


class Sem:
    __slots__ = ("h", "n", "name")


class Prog:
    ENGS = ("sync", "gpsimd", "tensor", "scalar", "vector")

    def __init__(self, nc, es):
        self.nc = nc
        self.es = es
        self.q = {e: [] for e in self.ENGS}
        self.waited = {e: {} for e in self.ENGS}
        self.auto_sems = {}

    def sem(self, name):
        s = Sem()
        s.h = self.es.enter_context(self.nc.semaphore(name))
        s.n = 0
        s.name = name
        return s

    def add(self, eng, fn, waits=(), inc=None, inck=1):
        wl = []
        wd = self.waited[eng]
        auto = self.auto_sems.get(eng)
        if auto is not None:
            if auto.n > 0:
                wl.append((auto.h, auto.n))
                wd[auto.name] = max(wd.get(auto.name, 0), auto.n)
            if inc is None:
                inc = auto
        for sv in waits:
            if sv is None:
                continue
            s, v = sv
            if v is None or v <= 0:
                continue
            if v > wd.get(s.name, 0):
                wl.append((s.h, v))
                wd[s.name] = v
        tgt = None
        if inc is not None:
            if inck == 16 and inc.n > 0:
                wl.append((inc.h, inc.n))
                wd[inc.name] = max(wd.get(inc.name, 0), inc.n)
            inc.n += inck
            tgt = inc.n

        def op(e, fn=fn, wl=wl, inc=inc, inck=inck):
            for (h, v) in wl:
                e.wait_ge(h, v)
            ins = fn(e)
            if inc is not None:
                ins.then_inc(inc.h, inck)

        self.q[eng].append(op)
        return tgt

    def flush(self):
        qs = {k: v for k, v in self.q.items() if v}
        if not qs:
            return
        with self.nc.Block() as blk:
            for name, ops in qs.items():
                def run(e, ops=ops):
                    for op in ops:
                        op(e)
                getattr(blk, name)(run)
        self.q = {e: [] for e in self.ENGS}


def _blk(w, kc, nc_):
    k, n = w.shape
    assert k == kc * 128 and n == nc_ * 128
    return np.ascontiguousarray(w.reshape(kc, 128, nc_, 128).transpose(2, 0, 1, 3))


def build_program():
    nc = bass.Bass("TRN2", target_bir_lowering=False)

    x_h = nc.dram_tensor("x_rot", [HID, TALL], F32, kind="ExternalInput")
    wq_h = nc.dram_tensor("wq_b", [NQ, NF, 128, 128], F32, kind="ExternalInput")
    wk_h = nc.dram_tensor("wk_b", [NKV, NF, 128, 128], F32, kind="ExternalInput")
    wv_h = nc.dram_tensor("wv_b", [NKV, NF, 128, 128], F32, kind="ExternalInput")
    wo_h = nc.dram_tensor("wo_b", [NF, NQ, 128, 128], F32, kind="ExternalInput")
    u1_h = nc.dram_tensor("u1_b", [NR, NF, 128, 128], F32, kind="ExternalInput")
    v1_h = nc.dram_tensor("v1_b", [NI, 2, NR, 128, 128], F32, kind="ExternalInput")
    u2_h = nc.dram_tensor("u2_b", [NI * NR, 128, 128], F32, kind="ExternalInput")
    v2_h = nc.dram_tensor("v2_b", [NF, NR, 128, 128], F32, kind="ExternalInput")
    b1_h = nc.dram_tensor("b1_t", [128, NH2], F32, kind="ExternalInput")
    b2_h = nc.dram_tensor("b2_t", [128, NF], F32, kind="ExternalInput")
    cc_h = nc.dram_tensor("cc_t", [128, TALL], BF16, kind="ExternalInput")
    ss_h = nc.dram_tensor("ss_t", [128, TALL], BF16, kind="ExternalInput")
    diag_h = nc.dram_tensor("diag_m", [128, 128], BF16, kind="ExternalInput")
    win_h = nc.dram_tensor("win_m", [128, 128], BF16, kind="ExternalInput")
    psw_h = nc.dram_tensor("pswap", [128, 128], BF16, kind="ExternalInput")
    idn_h = nc.dram_tensor("ident", [128, 128], BF16, kind="ExternalInput")
    boff_h = nc.dram_tensor("bias_off", [1, 32], F32, kind="ExternalInput")
    onr_h = nc.dram_tensor("ones_row", [1, 128], F32, kind="ExternalInput")
    onc_h = nc.dram_tensor("ones_col", [128, 1], BF16, kind="ExternalInput")

    out_h = nc.dram_tensor("out", [HID, TOWN], F32, kind="ExternalOutput")
    q_d = nc.dram_tensor("q_bounce", [NQ, 128, TOWN], BF16, kind="Internal")
    a_d = nc.dram_tensor("a_bounce", [NF, 128, TOWN], BF16, kind="Internal")
    x1_d = nc.dram_tensor("x1_bounce", [NF, 128, TOWN], BF16, kind="Internal")

    es = contextlib.ExitStack()
    with es:
        P = Prog(nc, es)

        def sb(name, shape, dt, st=None):
            return (st or es).enter_context(nc.sbuf_tensor(name, shape, dt))

        dG = P.sem("dG")
        dX = [P.sem("dX0"), P.sem("dX1"), P.sem("dX2"), P.sem("dX3")]
        dG0 = P.sem("dG0")
        dG1 = P.sem("dG1")
        dL = P.sem("dL")
        dL0 = P.sem("dL0")
        dL1 = P.sem("dL1")
        dO0 = P.sem("dO0")
        dO1 = P.sem("dO1")
        dT0 = P.sem("dT0")
        dT1 = P.sem("dT1")
        sT = P.sem("sT")
        sS = P.sem("sS")
        sV = P.sem("sV")
        sG = P.sem("sG")
        P.auto_sems = {"scalar": sS, "vector": sV, "gpsimd": sG}
        P.dma_sems = [dG, dG0, dG1, dL, dL0, dL1, dO0, dO1, dT0, dT1] + dX

        diag = sb("diag", [128, 128], BF16)
        win = sb("win", [128, 128], BF16)
        psw = sb("psw", [128, 128], BF16)
        idn = sb("idn", [128, 128], BF16)
        onc = sb("onc", [128, 1], BF16)
        onr = sb("onr", [1, 128], F32)
        onrb = sb("onrb", [1, 128], BF16)
        b1t = sb("b1t", [128, NH2], F32)
        b2t = sb("b2t", [128, NF], F32)
        boff = sb("boff", [1, 32], F32)
        btabA = sb("btabA", [128, 32], F32)
        btabB = sb("btabB", [128, 32], F32)
        m_row = sb("m_row", [1, TOWN], F32)
        v_row = sb("v_row", [1, TOWN], F32)
        r_row = sb("r_row", [1, TOWN], F32)
        mr_row = sb("mr_row", [1, TOWN], F32)
        rec_row = sb("rec_row", [1, TOWN], F32)
        r_bf = sb("r_bf", [1, TOWN], BF16)
        mr_bf = sb("mr_bf", [1, TOWN], BF16)
        rec_bf = mr_bf
        rec_scr = v_row
        qmax_row = sb("qmax_row", [1, NQ], F32)
        kmax_row = sb("kmax_row", [1, NKV], F32)
        nt = sb("nt", [1, 4], F32)
        srow = sb("srow", [1, 256], F32)
        grR = sb("grR", [128, TOWN], BF16)
        grMR = sb("grMR", [128, TOWN], BF16)
        epsc = sb("epsc", [128, 1], F32)
        z512 = sb("z512", [128, 512], BF16)
        tinyc = sb("tinyc", [128, 1], F32)

        PSA = es.enter_context(nc.psum_tensor("psa", [128, 3584], F32))
        MMS = [1024, 2048]
        STATS = 3072
        SWAP = 0

        _cl = [(onc, onc_h), (diag, diag_h), (win, win_h),
               (psw, psw_h), (idn, idn_h)]
        _csems = [dL, dO0, dO1, dT0, dT1]
        consts_w = []
        for i, (t, h) in enumerate(_cl + [(onr, onr_h), (b1t, b1_h), (b2t, b2_h), (boff, boff_h)]):
            s = _csems[i % len(_csems)]
            v = P.add("sync", lambda e, t=t, h=h: e.dma_start(t[:], h.ap()), inc=s, inck=16)
            consts_w.append((s, v))
        consts_w = [(s, max(v for s2, v in consts_w if s2 is s)) for s in _csems]
        mz1 = P.add("vector", lambda e: e.memset(epsc[:], EPS), inc=sV)
        P.add("vector", lambda e: e.memset(onrb[:], 1.0), inc=sV)
        mz0 = P.add("vector", lambda e: e.memset(z512[:], 0.0), inc=sV)
        mz2 = P.add("vector", lambda e: e.memset(tinyc[:], 1e-30), inc=sV)

        s1 = contextlib.ExitStack()
        es.enter_context(s1)
        k_fm = sb("k_fm", [128, NKV * TALL], BF16, s1)
        v_tm = sb("v_tm", [128, 16 * 1024], BF16, s1)

        sA = contextlib.ExitStack()
        es.enter_context(sA)
        y1 = sb("y1", [128, NF * 1024], BF16, sA)
        cc = sb("cc", [128, TALL], BF16, sA)
        ssn = sb("ssn", [128, TALL], BF16, sA)
        _ccv = P.add("sync", lambda e: e.dma_start(cc[:], cc_h.ap()), inc=dT0, inck=16)
        _ssv = P.add("sync", lambda e: e.dma_start(ssn[:], ss_h.ap()), inc=dT1, inck=16)
        P.add("vector", lambda e: e.nop(),
              waits=[(dT0, _ccv), (dT1, _ssv)] + consts_w)
        sbf = sb("sbf", [128, 3 * 2048], BF16, sA)
        xstg = {0: sbf[:, 2048:4096].bitcast(F32),
                1: sbf[:, 4096:6144].bitcast(F32)}
        xstgrel = {0: [], 1: []}
        scf = sb("scf", [128, 1024], F32, sA)
        scfb = sb("scfb", [128, 2 * 1024], BF16, sA)
        wbuf = sb("wbufA", [128, 2 * 4096], BF16, sA)
        PSB = sA.enter_context(nc.psum_tensor("psb", [128, 1024], BF16))
        P.add("tensor", lambda e: e.nop(), waits=consts_w + [(sV, mz2)])
        kv_pre = {}
        WSL = 4096

        def wsl(slot, j):
            return wbuf[:, slot * WSL + j * 128: slot * WSL + (j + 1) * 128]

        wrel = {0: [], 1: []}
        psrel = {0: [], 1: []}
        strel = {0: [], 1: []}
        sqrel = {0: [], 1: []}
        rorel = {0: [], 1: []}
        swaprel = []
        vtmrel = []
        nrmrel = []
        ntrel = []
        qstore = []

        def phase_A_half(half, do_q):
            base = half * 1024
            xwait = {}
            for fc in range(NF):
                src = x_h.ap()[fc * 128:(fc + 1) * 128, base:base + 1024]
                if fc % 2 == 0:
                    xs = dX[(fc // 2) % 4]
                    xwait[fc] = (xs, P.add("gpsimd", lambda e, fc=fc, src=src:
                          e.dma_start(y1[:, fc * 1024:(fc + 1) * 1024], src), inc=xs, inck=16))
                else:
                    ssl = (fc // 2) % 2
                    guard = strel[0] + strel[1] if ssl == 0 else sqrel[0] + sqrel[1]
                    lsem = dL if ssl == 0 else dO0
                    ld = P.add("sync", lambda e, src=src, ssl=ssl:
                               e.dma_start(xstg[ssl], src),
                               waits=xstgrel[ssl] + guard, inc=lsem, inck=16)
                    cpv = P.add("scalar", lambda e, fc=fc, ssl=ssl:
                                e.copy(y1[:, fc * 1024:(fc + 1) * 1024], xstg[ssl]),
                                waits=[(lsem, ld)], inc=sS)
                    xstgrel[ssl] = [(sS, cpv)]
                    xwait[fc] = (sS, cpv)
            if half == 0:
                for c in (0, 1):
                    wsrc = wk_h.ap()[c].rearrange("f p c -> p f c")
                    ssem = dG0 if c == 0 else dG1
                    kv_pre[c] = P.add("gpsimd", lambda e, wsrc=wsrc, c=c:
                                      e.dma_start(wbuf[:, c * WSL:(c + 1) * WSL]
                                                  .rearrange("p (f c) -> p f c", c=128), wsrc),
                                      inc=ssem, inck=16)
            sqv = {}
            mmv = {}
            for fc in range(NF):
                sl = fc % 2
                sqap = sbf[:, sl * 1024:(sl + 1) * 1024]
                sqv[fc] = P.add("vector", lambda e, fc=fc, sqap=sqap:
                                e.tensor_tensor(sqap, y1[:, fc * 1024:(fc + 1) * 1024],
                                                y1[:, fc * 1024:(fc + 1) * 1024], ALU.mult),
                                waits=[xwait[fc]] + rorel[sl], inc=sV)
                last = None
                for (p0, p1) in _pieces(0, 1024):
                    P.add("tensor", lambda e, fc=fc, p0=p0, p1=p1:
                          e.matmul(PSA[0:1, p0:p1], onc[:],
                                   y1[:, fc * 1024 + p0:fc * 1024 + p1],
                                   start=(fc == 0), stop=(fc == NF - 1)),
                          waits=[xwait[fc]], inc=None)
                    last = P.add("tensor", lambda e, sqap=sqap, p0=p0, p1=p1, fc=fc:
                                 e.matmul(PSA[0:1, 1024 + p0:1024 + p1], onc[:], sqap[:, p0:p1],
                                          start=(fc == 0), stop=(fc == NF - 1)),
                                 waits=[(sV, sqv[fc])], inc=sT)
                mmv[fc] = last
                rorel[sl] = [(sT, last)]
            stats_mm = mmv[NF - 1]
            a1 = P.add("scalar", lambda e: e.activation(m_row[:], PSA[0:1, 0:1024], AF.Copy, scale=1.0 / HID),
                       waits=[(sT, stats_mm)], inc=sS)
            a2 = P.add("scalar", lambda e: e.activation(v_row[:], PSA[0:1, 1024:2048], AF.Copy, scale=1.0 / HID),
                       inc=sS)
            t1 = P.add("vector", lambda e: e.tensor_tensor(mr_row[:], m_row[:], m_row[:], ALU.mult),
                       waits=[(sS, a2)], inc=sV)
            t2 = P.add("vector", lambda e: e.tensor_tensor(v_row[:], v_row[:], mr_row[:], ALU.subtract),
                       waits=[(sV, t1)], inc=sV)
            l1 = P.add("scalar", lambda e: e.activation(r_row[:], v_row[:], AF.Ln, bias=epsc[0:1, :]),
                       waits=[(sV, t2)] + consts_w, inc=sS)
            l2 = P.add("scalar", lambda e: e.activation(r_row[:], r_row[:], AF.Exp, scale=-0.5),
                       waits=[(sS, l1)], inc=sS)
            t3 = P.add("vector", lambda e: e.tensor_tensor(mr_row[:], m_row[:], r_row[:], ALU.mult),
                       waits=[(sS, l2)], inc=sV)
            stats_read = a2
            bcv = None
            for (p0, p1) in _pieces(0, 1024):
                P.add("tensor", lambda e, p0=p0, p1=p1:
                      e.matmul(PSA[:, 1024 + p0:1024 + p1], onr[:], r_row[:, p0:p1],
                               start=True, stop=True),
                      waits=[(sS, l2)] + psrel[0] + psrel[1], inc=None)
                bcv = P.add("tensor", lambda e, p0=p0, p1=p1:
                      e.matmul(PSA[:, 2048 + p0:2048 + p1], onr[:], mr_row[:, p0:p1],
                               start=True, stop=True),
                      waits=[(sV, t3)], inc=sT)
            g1 = P.add("scalar", lambda e: e.copy(grR[:], PSA[:, 1024:2048]),
                       waits=[(sT, bcv)], inc=sS)
            g2 = P.add("scalar", lambda e: e.copy(grMR[:], PSA[:, 2048:3072]), inc=sS)
            psrel[0] = [(sS, g1)]
            psrel[1] = [(sS, g2)]
            apv = {}
            for fc in range(NF):
                ch = y1[:, fc * 1024:(fc + 1) * 1024]
                sl = fc % 2
                sc = scfb[:, sl * 1024:(sl + 1) * 1024]
                w = [(sS, g2), (sV, sqv[fc]), (sT, mmv[fc])]
                if fc >= 2:
                    w.append((sV, apv[fc - 2]))
                P.add("vector", lambda e, ch=ch, sc=sc: e.tensor_tensor(sc, ch, grR[:], ALU.mult),
                      waits=w, inc=sV)
                apv[fc] = P.add("vector", lambda e, ch=ch, sc=sc:
                                e.tensor_tensor(ch, sc, grMR[:], ALU.subtract), inc=sV)
            y1d = apv[NF - 1]
            scfrel = [(sV, y1d)]

            iters = [("kv", c) for c in range(16)] + ([("q", i) for i in range(NQ)] if do_q else [])
            deferred = []
            nonlocal_state = {}

            for it_i, (kind, idx) in enumerate(iters):
                sl = it_i % 2
                ssem = dG0 if sl == 0 else dG1
                if kind == "kv":
                    wsrc = (wk_h if idx < 8 else wv_h).ap()[idx % 8].rearrange("f p c -> p f c")
                else:
                    wsrc = wq_h.ap()[idx].rearrange("f p c -> p f c")
                if half == 0 and it_i < 2:
                    wld = kv_pre[it_i]
                else:
                    wld = P.add("gpsimd", lambda e, wsrc=wsrc, sl=sl:
                                e.dma_start(wbuf[:, sl * WSL:(sl + 1) * WSL]
                                            .rearrange("p (f c) -> p f c", c=128), wsrc),
                                waits=wrel[sl], inc=ssem, inck=16)
                wrel[sl] = []
                pb = MMS[sl]
                pw = [(ssem, wld)] + psrel[sl]
                mml = None
                for fc in range(NF):
                    fw = list(pw) + ([(sV, apv[fc])] if it_i < 2 else [])
                    for (p0, p1) in _pieces(0, 1024):
                        mml = P.add("tensor",
                                    lambda e, fc=fc, p0=p0, p1=p1, pb=pb, sl=sl:
                                    e.matmul(PSA[:, pb + p0:pb + p1], wsl(sl, fc),
                                             y1[:, fc * 1024 + p0:fc * 1024 + p1],
                                             start=(fc == 0), stop=(fc == NF - 1)),
                                    waits=fw,
                                    inc=sT if (fc == NF - 1 and p1 == 1024) else None)
                        fw = ()
                        pw = ()
                wrel[sl] = [(sT, mml)]
                for dfn in deferred:
                    dfn()
                deferred = []
                stage = sbf[:, 2048 + sl * 1024:2048 + (sl + 1) * 1024]
                ev = P.add("scalar", lambda e, stage=stage, pb=pb: e.copy(stage, PSA[:, pb:pb + 1024]),
                           waits=[(sT, mml)] + strel[sl], inc=sS)
                psrel[sl] = [(sS, ev)]
                if kind == "kv" and idx >= 8:
                    c8 = idx - 8
                    def dv(stage=stage, c8=c8, half=half, ev=ev, sl=sl):
                        tpw = [(sS, ev)] + vtmrel[-1:] if vtmrel else [(sS, ev)]
                        tl = None
                        for tt in range(8):
                            tl = P.add("tensor", lambda e, stage=stage, tt=tt:
                                       e.transpose(PSB[:, tt * 128:(tt + 1) * 128],
                                                   stage[:, tt * 128:(tt + 1) * 128], idn[:]),
                                       waits=tpw if tt == 0 else (), inc=sT)
                            tpw = ()
                        vout = v_tm.ap().rearrange("p (t h d) -> p t h d", t=16, h=8, d=128)[
                            :, half * 8:half * 8 + 8, c8, :]
                        vin = PSB.ap().rearrange("p (t d) -> p t d", t=8, d=128)
                        vt = P.add("scalar", lambda e, vout=vout, vin=vin: e.copy(vout, vin),
                                   waits=[(sT, tl)], inc=sS)
                        vtmrel.append((sS, vt))
                        strel[sl] = [(sT, tl)]
                    deferred.append(dv)
                else:
                    sqs = sbf[:, 4096 + sl * 1024:4096 + (sl + 1) * 1024]
                    sq = P.add("scalar", lambda e, stage=stage, sqs=sqs:
                               e.activation(sqs, stage, AF.Square),
                               waits=sqrel[sl], inc=sS)
                    sqrel[sl] = []

                    def dk(stage=stage, sqs=sqs, sq=sq, kind=kind, idx=idx, sl=sl,
                           ev=ev, base=base, half=half):
                        p0v = P.add("tensor", lambda e, sqs=sqs:
                                    e.matmul(PSA[0:1, STATS:STATS + 512], onc[:],
                                             sqs[:, 0:512], start=True, stop=True),
                                    waits=[(sS, sq)] + nrmrel[-1:], inc=sT)
                        sww = [(sS, ev)] + swaprel[-1:]
                        swl = None
                        for (p0, p1) in _pieces(0, 1024):
                            swl = P.add("tensor", lambda e, stage=stage, p0=p0, p1=p1:
                                        e.matmul(PSA[:, SWAP + p0:SWAP + p1], psw[:],
                                                 stage[:, p0:p1], start=True, stop=True),
                                        waits=sww, inc=sT)
                            sww = ()
                        r0 = P.add("vector", lambda e:
                                   e.reduce_max(nt[0:1, 0:1], PSA[0:1, STATS:STATS + 512], axis=AX.X),
                                   waits=[(sT, p0v)] + ntrel[-1:], inc=sV)
                        p1v = P.add("tensor", lambda e, sqs=sqs:
                                    e.matmul(PSA[0:1, STATS:STATS + 512], onc[:],
                                             sqs[:, 512:1024], start=True, stop=True),
                                    waits=[(sV, r0)], inc=sT)
                        r1 = P.add("vector", lambda e:
                                   e.reduce_max(nt[0:1, 1:2], PSA[0:1, STATS:STATS + 512], axis=AX.X),
                                   waits=[(sT, p1v)], inc=sV)
                        nrmrel.append((sV, r1))
                        mx = P.add("vector", lambda e:
                                   e.tensor_tensor(nt[0:1, 2:3], nt[0:1, 0:1], nt[0:1, 1:2], ALU.max),
                                   inc=sV)
                        if kind == "kv":
                            if half == 0:
                                nv = P.add("vector", lambda e, idx=idx:
                                           e.tensor_copy(kmax_row[0:1, idx:idx + 1], nt[0:1, 2:3]),
                                           waits=[(sV, mx)], inc=sV)
                            else:
                                nv = P.add("vector", lambda e, idx=idx:
                                           e.tensor_tensor(kmax_row[0:1, idx:idx + 1],
                                                           kmax_row[0:1, idx:idx + 1], nt[0:1, 2:3], ALU.max),
                                           waits=[(sV, mx)], inc=sV)
                        else:
                            nv = P.add("vector", lambda e, idx=idx:
                                       e.tensor_copy(qmax_row[0:1, idx:idx + 1], nt[0:1, 2:3]),
                                       waits=[(sV, mx)], inc=sV)
                        ntrel.append((sV, nv))
                        sqrel[sl] = [(sT, p1v)]
                        if kind == "kv":
                            dst = k_fm[:, idx * TALL + base: idx * TALL + base + 1024]
                            tb = base
                            rowait = []
                        else:
                            dst = sbf[:, sl * 1024:(sl + 1) * 1024]
                            tb = 1024
                            rowait = rorel[sl]
                        m1 = P.add("vector", lambda e, stage=stage, tb=tb:
                                   e.tensor_tensor(scfb[:, 0:1024], stage, cc[:, tb:tb + 1024], ALU.mult),
                                   waits=[(sS, ev)] + scfrel, inc=sV)
                        m2 = P.add("vector", lambda e, tb=tb:
                                   e.tensor_tensor(scf[:, 0:1024], PSA[:, SWAP:SWAP + 1024],
                                                   ssn[:, tb:tb + 1024], ALU.mult),
                                   waits=[(sT, swl)], inc=sV)
                        swaprel.append((sV, m2))
                        rp = P.add("vector", lambda e, dst=dst:
                                   e.tensor_tensor(dst, scfb[:, 0:1024], scf[:, 0:1024], ALU.add),
                                   waits=rowait, inc=sV)
                        scfrel[:] = [(sV, rp)]
                        strel[sl] = [(sV, m1), (sT, swl), (sS, sq)]
                        if kind == "q":
                            osem = dO0 if sl == 0 else dO1
                            sv_ = P.add("scalar", lambda e, dst=dst, idx=idx:
                                        e.dma_start(q_d.ap()[idx], dst),
                                        waits=[(sV, rp)], inc=osem, inck=16)
                            rorel[sl] = [(osem, sv_)]
                            qstore.append((osem, sv_))
                    deferred.append(dk)

            for dfn in deferred:
                dfn()
            deferred = []


        phase_A_half(0, do_q=False)
        P.flush()
        phase_A_half(1, do_q=False)
        P.flush()

        def phase_A_q():
            deferred = []
            scfrel = [(sV, sV.n)]
            for hq in range(NQ):
                sl = hq % 2
                ssem = dG0 if sl == 0 else dG1
                wsrc = wq_h.ap()[hq].rearrange("f p c -> p f c")
                wld = P.add("gpsimd", lambda e, wsrc=wsrc, sl=sl:
                            e.dma_start(wbuf[:, sl * WSL:(sl + 1) * WSL]
                                        .rearrange("p (f c) -> p f c", c=128), wsrc),
                            waits=wrel[sl], inc=ssem, inck=16)
                wrel[sl] = []
                pb = MMS[sl]
                pw = [(ssem, wld)] + psrel[sl]
                mml = None
                for fc in range(NF):
                    for (p0, p1) in _pieces(0, 1024):
                        mml = P.add("tensor",
                                    lambda e, fc=fc, p0=p0, p1=p1, pb=pb, sl=sl:
                                    e.matmul(PSA[:, pb + p0:pb + p1], wsl(sl, fc),
                                             y1[:, fc * 1024 + p0:fc * 1024 + p1],
                                             start=(fc == 0), stop=(fc == NF - 1)),
                                    waits=pw if fc == 0 else (),
                                    inc=sT if (fc == NF - 1 and p1 == 1024) else None)
                        pw = ()
                wrel[sl] = [(sT, mml)]
                for dfn in deferred:
                    dfn()
                deferred = []
                stage = sbf[:, 2048 + sl * 1024:2048 + (sl + 1) * 1024]
                ev = P.add("scalar", lambda e, stage=stage, pb=pb: e.copy(stage, PSA[:, pb:pb + 1024]),
                           waits=[(sT, mml)] + strel[sl], inc=sS)
                psrel[sl] = [(sS, ev)]
                sqs = sbf[:, 4096 + sl * 1024:4096 + (sl + 1) * 1024]
                sq = P.add("scalar", lambda e, stage=stage, sqs=sqs:
                           e.activation(sqs, stage, AF.Square),
                           waits=sqrel[sl], inc=sS)
                sqrel[sl] = []

                def dq(stage=stage, sqs=sqs, sq=sq, hq=hq, sl=sl, ev=ev):
                    p0v = P.add("tensor", lambda e, sqs=sqs:
                                e.matmul(PSA[0:1, STATS:STATS + 512], onc[:],
                                         sqs[:, 0:512], start=True, stop=True),
                                waits=[(sS, sq)] + nrmrel[-1:], inc=sT)
                    sww = [(sS, ev)] + swaprel[-1:]
                    swl = None
                    for (p0, p1) in _pieces(0, 1024):
                        swl = P.add("tensor", lambda e, stage=stage, p0=p0, p1=p1:
                                    e.matmul(PSA[:, SWAP + p0:SWAP + p1], psw[:],
                                             stage[:, p0:p1], start=True, stop=True),
                                    waits=sww, inc=sT)
                        sww = ()
                    r0 = P.add("vector", lambda e:
                               e.reduce_max(nt[0:1, 0:1], PSA[0:1, STATS:STATS + 512], axis=AX.X),
                               waits=[(sT, p0v)] + ntrel[-1:], inc=sV)
                    p1v = P.add("tensor", lambda e, sqs=sqs:
                                e.matmul(PSA[0:1, STATS:STATS + 512], onc[:],
                                         sqs[:, 512:1024], start=True, stop=True),
                                waits=[(sV, r0)], inc=sT)
                    r1 = P.add("vector", lambda e:
                               e.reduce_max(nt[0:1, 1:2], PSA[0:1, STATS:STATS + 512], axis=AX.X),
                               waits=[(sT, p1v)], inc=sV)
                    nrmrel.append((sV, r1))
                    mx = P.add("vector", lambda e:
                               e.tensor_tensor(nt[0:1, 2:3], nt[0:1, 0:1], nt[0:1, 1:2], ALU.max),
                               inc=sV)
                    nv = P.add("vector", lambda e, hq=hq:
                               e.tensor_copy(qmax_row[0:1, hq:hq + 1], nt[0:1, 2:3]),
                               waits=[(sV, mx)], inc=sV)
                    ntrel.append((sV, nv))
                    sqrel[sl] = [(sT, p1v)]
                    ro = sbf[:, sl * 1024:(sl + 1) * 1024]
                    m1 = P.add("vector", lambda e, stage=stage:
                               e.tensor_tensor(scfb[:, 0:1024], stage, cc[:, 1024:2048], ALU.mult),
                               waits=[(sS, ev)] + scfrel, inc=sV)
                    m2 = P.add("vector", lambda e:
                               e.tensor_tensor(scf[:, 0:1024], PSA[:, SWAP:SWAP + 1024],
                                               ssn[:, 1024:2048], ALU.mult),
                               waits=[(sT, swl)], inc=sV)
                    swaprel.append((sV, m2))
                    rp = P.add("vector", lambda e, ro=ro:
                               e.tensor_tensor(ro, scfb[:, 0:1024], scf[:, 0:1024], ALU.add),
                               waits=rorel[sl], inc=sV)
                    scfrel[:] = [(sV, rp)]
                    strel[sl] = [(sV, m1), (sT, swl), (sS, sq)]
                    osem = dO0 if sl == 0 else dO1
                    sv_ = P.add("scalar", lambda e, ro=ro, hq=hq: e.dma_start(q_d.ap()[hq], ro),
                                waits=[(sV, rp)], inc=osem, inck=16)
                    rorel[sl] = [(osem, sv_)]
                    qstore.append((osem, sv_))
                deferred.append(dq)
            for dfn in deferred:
                dfn()

        phase_A_q()
        P.flush()

        lq = P.add("scalar", lambda e: e.activation(srow[0:1, 0:32], qmax_row[:], AF.Ln, bias=tinyc[0:1, :]),
                   waits=ntrel[-1:], inc=sS)
        lk = P.add("scalar", lambda e: e.activation(srow[0:1, 32:40], kmax_row[:], AF.Ln, bias=tinyc[0:1, :]),
                   inc=sS)
        sm_last = None
        for j in range(4):
            sm_last = P.add("vector", lambda e, j=j: e.tensor_tensor(
                srow.ap().rearrange("p (a k g) -> p a k g", a=8, k=8, g=4)[:, 2, :, j],
                srow.ap().rearrange("p (a k g) -> p a k g", a=8, k=8, g=4)[:, 0, :, j],
                srow[0:1, 32:40], ALU.add),
                waits=[(sS, lk)] + ([(sV, sm_last)] if sm_last else []), inc=sV)
        ex1 = P.add("scalar", lambda e: e.activation(srow[0:1, 96:128], srow[0:1, 64:96], AF.Exp, scale=0.5),
                    waits=[(sV, sm_last)], inc=sS)
        ngv = P.add("scalar", lambda e: e.activation(srow[0:1, 128:160], srow[0:1, 96:128], AF.Copy, scale=-SM),
                    waits=[(sS, ex1)], inc=sS)
        z4 = P.add("vector", lambda e: e.tensor_tensor(srow[0:1, 160:192], srow[0:1, 128:160], boff[:], ALU.add),
                   waits=[(sS, ngv)] + consts_w, inc=sV)
        P.add("tensor", lambda e: e.matmul(PSA[:, 128:160], onr[:], srow[0:1, 128:160], start=True, stop=True),
              waits=[(sV, z4)] + swaprel[-1:] + nrmrel[-1:], inc=None)
        bb = P.add("tensor", lambda e: e.matmul(PSA[:, 160:192], onr[:], srow[0:1, 160:192], start=True, stop=True),
                   inc=sT)
        btv1 = P.add("scalar", lambda e: e.copy(btabB[:], PSA[:, 128:160]), waits=[(sT, bb)], inc=sS)
        btv2 = P.add("scalar", lambda e: e.copy(btabA[:], PSA[:, 160:192]), inc=sS)
        if DEV_STOP == "A":
            P.flush()
            _dump(nc, P, dG, [("y1", y1), ("k_fm", k_fm), ("v_tm", v_tm),
                              ("btabA", btabA), ("btabB", btabB)])
            return nc
        P.flush()
        sA.close()
        PSC = es.enter_context(nc.psum_tensor("psc", [128, 512], F32))

        sBo = contextlib.ExitStack()
        es.enter_context(sBo)
        attn = sb("attn", [128, NQ * 1024], BF16, sBo)
        wbufC = sb("wbufC", [128, 2 * 4096], BF16, sBo)
        sBi = contextlib.ExitStack()
        es.enter_context(sBi)
        Ebuf = sb("Ebuf", [128, E_TOT], BF16, sBi)
        qsl = sb("qsl", [128, 2 * 1024], BF16, sBi)
        recbc = sb("recbc", [128, 1024], BF16, sBi)

        AV0 = 2048
        head_pe = {}
        mask_last = {}
        expd = {}
        qld = {}
        mo_prev = {}
        rcc_prev = {}
        rec_read_prev = {}

        def b_norm_chain(hq, avv):
            P.add("vector", lambda e: e.reciprocal(rec_row[0:1, 0:512], PSA[0:1, STATS:STATS + 512]),
                  waits=[(sT, avv)], inc=sV)
            rcv = P.add("vector", lambda e: e.tensor_copy(rec_bf[0:1, 0:512], rec_row[0:1, 0:512]),
                        inc=sV)
            P.add("scalar", lambda e: e.activation(rec_row[0:1, 512:1024], PSC[0:1, 0:512], AF.Ln),
                  waits=[(sT, avv)], inc=sS)
            rc2 = P.add("scalar", lambda e: e.activation(rec_bf[0:1, 512:1024], rec_row[0:1, 512:1024],
                                                         AF.Exp, scale=-1.0), inc=sS)
            P.add("tensor", lambda e: e.matmul(PSA[:, STATS:STATS + 512], onrb[:], rec_bf[0:1, 0:512],
                                               start=True, stop=True),
                  waits=[(sV, rcv), (sS, rc2)], inc=None)
            bcv = P.add("tensor", lambda e: e.matmul(PSC[:, 0:512], onrb[:], rec_bf[0:1, 512:1024],
                                                     start=True, stop=True), inc=sT)
            w = [(sT, bcv)]
            if hq >= 1:
                w.append((sV, mo_prev[hq - 1]))
            P.add("vector", lambda e: e.tensor_copy(recbc[:, 0:512], PSA[:, STATS:STATS + 512]),
                  waits=w, inc=sV)
            rcc = P.add("vector", lambda e: e.tensor_copy(recbc[:, 512:1024], PSC[:, 0:512]), inc=sV)
            mo = P.add("vector", lambda e, hq=hq:
                       e.tensor_tensor(attn[:, hq * 1024:(hq + 1) * 1024],
                                       PSA[:, AV0:AV0 + 1024], recbc[:], ALU.mult),
                       waits=[(sV, rcc)], inc=sV)
            mo_prev[hq] = mo
            rcc_prev[hq] = rcc
            rec_read_prev[hq] = (rcv, rc2)

        wo_pre = {}
        for oc in (0, 1):
            wsrc = wo_h.ap()[oc].rearrange("f p c -> p f c")
            ssem = dG0 if oc == 0 else dG1
            wo_pre[oc] = P.add("gpsimd", lambda e, wsrc=wsrc, oc=oc:
                               e.dma_start(wbufC[:, oc * 4096:(oc + 1) * 4096]
                                           .rearrange("p (f c) -> p f c", c=128), wsrc),
                               waits=list(qstore), inc=ssem, inck=16)

        B_JOBS = [(0, 1), (2, 3), (4,), (5,), (6,), (7,), (8,), (9,), (10,),
                  (11,), (12, 13), (14, 15)]
        half_guard = {0: 0, 1: 0}
        for hq in range(NQ):
            kh = hq // G
            sl = hq % 2
            qsem = dL0 if sl == 0 else dL1
            w = list(qstore)
            if hq >= 2:
                w.append((sT, head_pe[hq - 2]))
            qld[hq] = P.add("sync", lambda e, hq=hq, sl=sl:
                            e.dma_start(qsl[:, sl * 1024:(sl + 1) * 1024], q_d.ap()[hq]),
                            waits=w, inc=qsem, inck=16)
            mkv = {}
            for ji in range(len(B_JOBS) + 3):
                if ji < len(B_JOBS):
                    job = B_JOBS[ji]
                    half = ji % 2
                    sb0 = half * 1024
                    swt = [(qsem, qld[hq]), (sS, half_guard[half])]
                    if hq == 0 and ji < 2:
                        swt.append((sS, btv2))
                    if hq >= 1 and ji < 2:
                        swt += [(sT, head_pe[hq - 1]), (sG, mask_last[hq - 1])]
                    off = 0
                    scl = None
                    wtot = 0
                    for st in job:
                        qlo, qhi, eo = STRIPS[st]
                        for (p0, p1) in _pieces(off, off + qhi - qlo):
                            scl = P.add("tensor",
                                        lambda e, st=st, p0=p0, p1=p1, sb0=sb0, kh=kh, sl=sl, off=off, qlo=qlo:
                                        e.matmul(PSA[:, sb0 + p0:sb0 + p1],
                                                 k_fm[:, kh * TALL + st * 128:kh * TALL + (st + 1) * 128],
                                                 qsl[:, sl * 1024 + qlo + (p0 - off):sl * 1024 + qlo + (p1 - off)],
                                                 start=True, stop=True),
                                        waits=swt, inc=sT)
                            swt = ()
                        off += qhi - qlo
                    wtot = off
                    bt = btabA if job[0] < 8 else btabB
                    eo0 = STRIPS[job[0]][2]
                    half_guard[half] = P.add(
                        "scalar", lambda e, wtot=wtot, eo0=eo0, sb0=sb0, bt=bt, hq=hq:
                        e.activation(Ebuf[:, eo0:eo0 + wtot], PSA[:, sb0:sb0 + wtot],
                                     AF.Exp, bias=bt[:, hq:hq + 1], scale=SM),
                        waits=[(sT, scl), (sS, btv2)], inc=sS)
                    for st in job:
                        qlo, qhi, eo = STRIPS[st]
                        if st >= 8:
                            mt, mo_ = diag, (st - 8) * 128 - qlo
                        else:
                            mt, mo_ = win, st * 128 - qlo
                        mkv[st] = P.add("gpsimd", lambda e, eo=eo, mo_=mo_, mt=mt:
                                        e.tensor_tensor(Ebuf[:, eo + mo_:eo + mo_ + 128],
                                                        Ebuf[:, eo + mo_:eo + mo_ + 128], mt[:], ALU.mult),
                                        waits=[(sS, half_guard[half])], inc=sG)
                if ji == 1:
                    if hq >= 1:
                        b_norm_chain(hq - 1, head_pe[hq - 1])
                    ow = [(sV, rcc_prev.get(hq - 1))] if hq >= 1 else []
                    P.add("tensor", lambda e: e.matmul(PSA[0:1, STATS:STATS + 512], onc[:], z512[:],
                                                       start=True, stop=False),
                          waits=[sv for sv in ow if sv[1] is not None], inc=None)
                    P.add("tensor", lambda e: e.matmul(PSC[0:1, 0:512], onc[:], z512[:],
                                                       start=True, stop=False), inc=None)
                    ow2 = [(sV, mo_prev.get(hq - 1))] if hq >= 1 else []
                    P.add("tensor", lambda e: e.matmul(PSA[:, AV0:AV0 + 512], psw[:], z512[:],
                                                       start=True, stop=False),
                          waits=[sv for sv in ow2 if sv[1] is not None], inc=None)
                    P.add("tensor", lambda e: e.matmul(PSA[:, AV0 + 512:AV0 + 1024], psw[:], z512[:],
                                                       start=True, stop=False), inc=None)
                if ji >= 3:
                    for st in B_JOBS[ji - 3]:
                        qlo, qhi, eo = STRIPS[st]
                        dw = [(sG, mkv[st])]
                        for (p0, p1) in _pieces(qlo, qhi):
                            lo = p0 < 512
                            tgt = (PSA[0:1, STATS + p0:STATS + p1] if lo
                                   else PSC[0:1, p0 - 512:p1 - 512])
                            dstop = (st == 11) if lo else (st == 15)
                            P.add("tensor", lambda e, tgt=tgt, p0=p0, p1=p1, eo=eo, qlo=qlo, dstop=dstop:
                                  e.matmul(tgt, onc[:], Ebuf[:, eo + p0 - qlo:eo + p1 - qlo],
                                           start=False, stop=dstop),
                                  waits=dw, inc=None)
                            dw = ()
                        for (p0, p1) in _pieces(qlo, qhi):
                            astop = (st == 11 and p1 == 512) or (st == 15 and p1 == 1024)
                            islast = (st == 15 and p1 == qhi)
                            v_ = P.add("tensor", lambda e, st=st, p0=p0, p1=p1, eo=eo, qlo=qlo, kh=kh, astop=astop:
                                  e.matmul(PSA[:, AV0 + p0:AV0 + p1],
                                           v_tm[:, st * 1024 + kh * 128:st * 1024 + (kh + 1) * 128],
                                           Ebuf[:, eo + p0 - qlo:eo + (p1 - qlo)],
                                           start=False, stop=astop),
                                  waits=(), inc=sT if islast else None)
                            if islast:
                                avv = v_
            mask_last[hq] = mkv[15]
            head_pe[hq] = avv
        b_norm_chain(NQ - 1, head_pe[NQ - 1])
        prev_rec_v, prev_rec_s = rec_read_prev[NQ - 1]
        if DEV_STOP == "B":
            P.flush()
            _dump(nc, P, dG, [("attn", attn)])
            return nc
        sBi.close()

        sC1 = contextlib.ExitStack()
        es.enter_context(sC1)
        astg = sb("astg", [128, 2 * 1024], BF16, sC1)
        asq = sb("asq", [128, 2 * 1024], BF16, sC1)

        wrelC = {0: [], 1: []}
        psrelC = {0: [], 1: []}
        strelC = {0: [], 1: []}
        sqrelC1 = {0: [], 1: []}
        stats_last = None
        for oc in range(NF):
            sl = oc % 2
            ssem = dG0 if sl == 0 else dG1
            if oc < 2:
                wld = wo_pre[oc]
            else:
                wsrc = wo_h.ap()[oc].rearrange("f p c -> p f c")
                wld = P.add("gpsimd", lambda e, wsrc=wsrc, sl=sl:
                            e.dma_start(wbufC[:, sl * 4096:(sl + 1) * 4096].rearrange("p (f c) -> p f c", c=128), wsrc),
                            waits=wrelC[sl], inc=ssem, inck=16)
            wrelC[sl] = []
            pb = MMS[sl]
            pw = [(ssem, wld)] + psrelC[sl]
            if oc == 0:
                pw.append((sV, prev_rec_v))
                pw.append((sS, prev_rec_s))
                pw.append((sS, half_guard[1]))
            if oc == 1:
                pw.append((sV, mo_prev[NQ - 1]))
            mml = None
            for hc in range(NF):
                fw = list(pw) + ([(sV, mo_prev[hc])] if oc < 2 else [])
                for (p0, p1) in _pieces(0, 1024):
                    mml = P.add("tensor", lambda e, hc=hc, p0=p0, p1=p1, pb=pb, sl=sl:
                                e.matmul(PSA[:, pb + p0:pb + p1],
                                         wbufC[:, sl * 4096 + hc * 128:sl * 4096 + (hc + 1) * 128],
                                         attn[:, hc * 1024 + p0:hc * 1024 + p1],
                                         start=(hc == 0), stop=(hc == NF - 1)),
                                waits=fw,
                                inc=sT if (hc == NF - 1 and p1 == 1024) else None)
                    fw = ()
                    pw = ()
            wrelC[sl] = [(sT, mml)]
            stg = astg[:, sl * 1024:(sl + 1) * 1024]
            ev = P.add("scalar", lambda e, stg=stg, pb=pb: e.copy(stg, PSA[:, pb:pb + 1024]),
                       waits=[(sT, mml)] + strelC[sl], inc=sS)
            psrelC[sl] = [(sS, ev)]
            sqs = asq[:, sl * 1024:(sl + 1) * 1024]
            sq = P.add("scalar", lambda e, stg=stg, sqs=sqs: e.activation(sqs, stg, AF.Square),
                       waits=sqrelC1[sl], inc=sS)
            sqrelC1[sl] = []
            stl = None
            for pi, (p0, p1) in enumerate(_pieces(0, 1024)):
                sumtgt = PSA[0:1, p0:p1]
                sqtgt = PSA[0:1, STATS:STATS + 512] if pi == 0 else PSC[0:1, 0:512]
                P.add("tensor", lambda e, stg=stg, p0=p0, p1=p1, sumtgt=sumtgt, oc=oc:
                      e.matmul(sumtgt, onc[:], stg[:, p0:p1],
                               start=(oc == 0), stop=(oc == NF - 1)),
                      waits=[(sS, ev)], inc=None)
                stl = P.add("tensor", lambda e, sqs=sqs, p0=p0, p1=p1, sqtgt=sqtgt, oc=oc:
                            e.matmul(sqtgt, onc[:], sqs[:, p0:p1],
                                     start=(oc == 0), stop=(oc == NF - 1)),
                            waits=[(sS, sq)], inc=sT)
            stats_last = stl
            sqrelC1[sl] = [(sT, stl)]
            osem = dO0 if sl == 0 else dO1
            sv_ = P.add("scalar", lambda e, stg=stg, oc=oc: e.dma_start(a_d.ap()[oc], stg),
                        waits=[(sS, sq)], inc=osem, inck=16)
            strelC[sl] = [(osem, sv_), (sT, stl), (sS, sq)]
        a_store = [(dO0, dO0.n), (dO1, dO1.n)]
        P.flush()
        sBi.close()
        sC1.close()
        sBo.close()
        s1.close()

        sC = contextlib.ExitStack()
        es.enter_context(sC)
        slab0 = sb("slab0", [128, NF * 1024], BF16, sC)
        wbufD = sb("wbufD", [128, 2 * 4096], BF16, sC)
        xfE = None

        def ln_rows(stats_wait, extra=()):
            c1 = P.add("scalar", lambda e: e.activation(m_row[:, 0:1024], PSA[0:1, 0:1024],
                                                        AF.Copy, scale=1.0 / HID),
                       waits=[(sT, stats_wait)] + list(extra), inc=sS)
            c3 = P.add("scalar", lambda e: e.activation(v_row[:, 0:512], PSA[0:1, STATS:STATS + 512],
                                                        AF.Copy, scale=1.0 / HID), inc=sS)
            c4 = P.add("scalar", lambda e: e.activation(v_row[:, 512:1024], PSC[0:1, 0:512],
                                                        AF.Copy, scale=1.0 / HID), inc=sS)
            t1 = P.add("vector", lambda e: e.tensor_tensor(mr_row[:], m_row[:], m_row[:], ALU.mult),
                       waits=[(sS, c4)], inc=sV)
            t2 = P.add("vector", lambda e: e.tensor_tensor(v_row[:], v_row[:], mr_row[:], ALU.subtract),
                       waits=[(sV, t1)], inc=sV)
            l1 = P.add("scalar", lambda e: e.activation(r_row[:], v_row[:], AF.Ln, bias=epsc[0:1, :]),
                       waits=[(sV, t2)], inc=sS)
            l2 = P.add("scalar", lambda e: e.activation(r_row[:], r_row[:], AF.Exp, scale=-0.5),
                       waits=[(sS, l1)], inc=sS)
            t3 = P.add("vector", lambda e: e.tensor_tensor(mr_row[:], m_row[:], r_row[:], ALU.mult),
                       waits=[(sS, l2)], inc=sV)
            bcv = None
            for (p0, p1) in _pieces(0, 1024):
                P.add("tensor", lambda e, p0=p0, p1=p1:
                      e.matmul(PSA[:, 1024 + p0:1024 + p1], onr[:], r_row[:, p0:p1],
                               start=True, stop=True),
                      waits=[(sS, l2)], inc=None)
                bcv = P.add("tensor", lambda e, p0=p0, p1=p1:
                      e.matmul(PSA[:, 2048 + p0:2048 + p1], onr[:], mr_row[:, p0:p1],
                               start=True, stop=True),
                      waits=[(sV, t3)], inc=sT)
            g1 = P.add("scalar", lambda e: e.copy(grR[:], PSA[:, 1024:2048]),
                       waits=[(sT, bcv)], inc=sS)
            g2 = P.add("scalar", lambda e: e.copy(grMR[:], PSA[:, 2048:3072]), inc=sS)
            return g2

        gC = ln_rows(stats_last)

        sC3 = contextlib.ExitStack()
        es.enter_context(sC3)
        NSL = 6
        ach = sb("ach", [128, NSL * 1024], BF16, sC3)
        scCb = sb("scCb", [128, NSL * 1024], BF16, sC3)
        xfs = sb("xfs", [128, NSL * 1024], F32, sC3)
        sqC = sb("sqC", [128, 2 * 1024], BF16, sC3)

        arelC = {i: [] for i in range(NSL)}
        xrelC = {i: [] for i in range(NSL)}
        sqrelC3 = {0: [], 1: []}
        st2_cell = [None]
        frontC = {}

        def c3_front(oc):
            sl = oc % NSL
            asem = dL0 if oc % 2 == 0 else dL1
            xsem = dT0 if oc % 2 == 0 else dT1
            av = P.add("sync", lambda e, oc=oc, sl=sl:
                       e.dma_start(ach[:, sl * 1024:(sl + 1) * 1024], a_d.ap()[oc]),
                       waits=arelC[sl] + (a_store if oc < 2 else []), inc=asem, inck=16)
            xv = P.add("scalar", lambda e, oc=oc, sl=sl:
                       e.dma_start(xfs[:, sl * 1024:(sl + 1) * 1024],
                                   x_h.ap()[oc * 128:(oc + 1) * 128, 1024:2048]),
                       waits=xrelC[sl], inc=xsem, inck=16)
            sb_ = scCb[:, sl * 1024:(sl + 1) * 1024]
            ee = "vector"
            esem = sV
            d1 = P.add(ee, lambda e, sb_=sb_, sl=sl:
                       e.tensor_tensor(sb_, ach[:, sl * 1024:(sl + 1) * 1024], grR[:], ALU.mult),
                       waits=[(asem, av), (sS, gC)], inc=esem)
            P.add(ee, lambda e, sb_=sb_: e.tensor_tensor(sb_, sb_, grMR[:], ALU.subtract), inc=esem)
            arelC[sl] = [(esem, d1)]
            frontC[oc] = (sl, sb_, ee, esem, xsem, xv)

        def c3_back(oc):
            sl, sb_, ee, esem, xsem, xv = frontC[oc]
            s0 = slab0[:, oc * 1024:(oc + 1) * 1024]
            d3 = P.add(ee, lambda e, s0=s0, sb_=sb_, sl=sl:
                       e.tensor_tensor(s0, sb_, xfs[:, sl * 1024:(sl + 1) * 1024], ALU.add),
                       waits=[(xsem, xv)], inc=esem)
            xrelC[sl] = [(esem, d3)]
            ssl = oc % 2
            osem = dG0 if ssl == 0 else dG1
            P.add("gpsimd", lambda e, s0=s0, oc=oc: e.dma_start(x1_d.ap()[oc], s0),
                  waits=[(esem, d3)], inc=osem, inck=16)
            sqs = sqC[:, ssl * 1024:(ssl + 1) * 1024]
            sq = P.add("scalar", lambda e, sqs=sqs, s0=s0:
                       e.activation(sqs, s0, AF.Square),
                       waits=sqrelC3[ssl] + [(esem, d3)], inc=sS)
            sqrelC3[ssl] = []
            stl = None
            for pi, (p0, p1) in enumerate(_pieces(0, 1024)):
                sumtgt = PSA[0:1, p0:p1]
                sqtgt = PSA[0:1, STATS:STATS + 512] if pi == 0 else PSC[0:1, 0:512]
                P.add("tensor", lambda e, oc=oc, p0=p0, p1=p1, sumtgt=sumtgt:
                      e.matmul(sumtgt, onc[:],
                               slab0[:, oc * 1024 + p0:oc * 1024 + p1],
                               start=(oc == 0), stop=(oc == NF - 1)),
                      waits=[(esem, d3)], inc=None)
                stl = P.add("tensor", lambda e, sqs=sqs, p0=p0, p1=p1, sqtgt=sqtgt, oc=oc:
                            e.matmul(sqtgt, onc[:], sqs[:, p0:p1],
                                     start=(oc == 0), stop=(oc == NF - 1)),
                            waits=[(sS, sq)], inc=sT)
            st2_cell[0] = stl
            sqrelC3[ssl] = [(sT, stl)]

        u1_pre = {}
        for r in (0, 1):
            wsrc = u1_h.ap()[r].rearrange("f p c -> p f c")
            ssem = dG0 if r == 0 else dG1
            u1_pre[r] = P.add("gpsimd", lambda e, wsrc=wsrc, r=r:
                              e.dma_start(wbufD[:, r * 4096:(r + 1) * 4096]
                                          .rearrange("p (f c) -> p f c", c=128), wsrc),
                              waits=list(a_store), inc=ssem, inck=16)
        SKEW = 2
        for oc in range(NF):
            c3_front(oc)
            if oc >= SKEW:
                c3_back(oc - SKEW)
        for oc in range(NF - SKEW, NF):
            c3_back(oc)
        st2_last = st2_cell[0]
        x1_store = [(dG0, dG0.n), (dG1, dG1.n)]
        for _eng in ("tensor", "scalar", "vector", "gpsimd"):
            P.add(_eng, lambda e: e.nop(), waits=x1_store)
        P.flush()
        sC3.close()

        gC2 = ln_rows(st2_last)
        yvs = {}
        sCy = contextlib.ExitStack()
        es.enter_context(sCy)
        scY = sb("scY", [128, 4 * 1024], BF16, sCy)
        yrel = {0: [], 1: [], 2: [], 3: []}
        for oc in range(NF):
            sl = oc % 4
            s0 = scY[:, sl * 1024:(sl + 1) * 1024]
            ee = "vector"
            esem = sV
            P.add(ee, lambda e, s0=s0, oc=oc:
                  e.tensor_tensor(s0, slab0[:, oc * 1024:(oc + 1) * 1024], grR[:], ALU.mult),
                  waits=[(sS, gC2)] + yrel[sl], inc=esem)
            yv = P.add(ee, lambda e, s0=s0, oc=oc:
                       e.tensor_tensor(slab0[:, oc * 1024:(oc + 1) * 1024], s0, grMR[:], ALU.subtract),
                       inc=esem)
            yrel[sl] = [(esem, yv)]
            yvs[oc] = (esem, yv)
        sCy.close()

        if DEV_STOP == "C":
            _dump(nc, P, dG, [("slab0", slab0)])
            return nc

        sD = contextlib.ExitStack()
        es.enter_context(sD)
        Zsb = sb("Zsb", [128, NR * 1024], F32, sD)
        Zbf = sb("Zbf", [128, NR * 1024], BF16, sD)
        Pbf = sb("Pbf", [128, NR * 1024], BF16, sD)
        wu2 = sb("wu2", [128, 4 * (GSZ * 128)], BF16, sD)
        wv1 = wbufD

        sD1 = contextlib.ExitStack()
        es.enter_context(sD1)
        wrelD = {0: [], 1: []}
        psrelD = {0: [], 1: []}
        for r in range(NR):
            sl = r % 2
            ssem = dG0 if sl == 0 else dG1
            if r < 2:
                wld = u1_pre[r]
            else:
                wsrc = u1_h.ap()[r].rearrange("f p c -> p f c")
                wld = P.add("gpsimd", lambda e, wsrc=wsrc, sl=sl:
                            e.dma_start(wbufD[:, sl * 4096:(sl + 1) * 4096].rearrange("p (f c) -> p f c", c=128), wsrc),
                            waits=wrelD[sl], inc=ssem, inck=16)
            wrelD[sl] = []
            pb = MMS[sl]
            pw = [(ssem, wld)] + psrelD[sl]
            mml = None
            for fc in range(NF):
                fw = list(pw) + ([yvs[fc]] if r < 2 else [])
                for (p0, p1) in _pieces(0, 1024):
                    mml = P.add("tensor", lambda e, fc=fc, p0=p0, p1=p1, pb=pb, sl=sl:
                                e.matmul(PSA[:, pb + p0:pb + p1],
                                         wbufD[:, sl * 4096 + fc * 128:sl * 4096 + (fc + 1) * 128],
                                         slab0[:, fc * 1024 + p0:fc * 1024 + p1],
                                         start=(fc == 0), stop=(fc == NF - 1)),
                                waits=fw,
                                inc=sT if (fc == NF - 1 and p1 == 1024) else None)
                    fw = ()
                    pw = ()
            wrelD[sl] = [(sT, mml)]
            ev = P.add("scalar", lambda e, r=r, pb=pb:
                       e.copy(Pbf[:, r * 1024:(r + 1) * 1024], PSA[:, pb:pb + 1024]),
                       waits=[(sT, mml)], inc=sS)
            psrelD[sl] = [(sS, ev)]
        pbf_done = sS.n
        v1_pre = {}
        for g in (0, 1):
            wsrc = v1_h.ap()[g].rearrange("a r p c -> p a r c")
            ssem = dG0 if g == 0 else dG1
            v1_pre[g] = P.add("gpsimd", lambda e, wsrc=wsrc, g=g:
                              e.dma_start(wv1[:, g * 2048:(g + 1) * 2048]
                                          .rearrange("p (a r c) -> p a r c", a=2, c=128), wsrc),
                              waits=wrelD[0] + wrelD[1], inc=ssem, inck=16)
        sD1.close()

        sD2 = contextlib.ExitStack()
        es.enter_context(sD2)
        actb = sb("actb", [128, GSZ * 1024], BF16, sD2)
        sgst = sb("sgst", [128, 2 * 512], BF16, sD2)
        ubst = sb("ubst", [128, 2 * 512], BF16, sD2)

        HS = [1024, 1536, 2048, 2560]
        wrel1 = {0: [], 1: []}
        wrel2 = {0: [], 1: [], 2: [], 3: []}
        hrel = {}
        sgrel = {0: [], 1: []}
        ubrel = {0: [], 1: []}
        zrel = {0: [], 1: []}
        zver = {}
        zfin = {}
        z_prev_group = [0]
        u2_off = [0]
        groups = _ffn_groups()
        NGRP = len(groups)

        def make_z(gi, g0, g1, actdv):
            gsz = g1 - g0

            def emit_z():
                zlast = None
                for r in range(NR):
                    sl2 = r % 4
                    ssem = dG0 if r % 2 == 0 else dG1
                    srcz = u2_h.ap()[u2_off[0]:u2_off[0] + gsz].rearrange("i p c -> p i c")
                    u2_off[0] += gsz
                    wld = P.add("gpsimd", lambda e, srcz=srcz, sl2=sl2, gsz=gsz:
                                e.dma_start(wu2[:, sl2 * GSZ * 128:sl2 * GSZ * 128 + gsz * 128]
                                            .rearrange("p (i c) -> p i c", c=128), srcz),
                                waits=wrel2[sl2], inc=ssem, inck=16)
                    wrel2[sl2] = []
                    lastz = None
                    for tp in range(2):
                        zsl = (r * 2 + tp) % 2
                        pw = [(ssem, wld)] + zrel[zsl]
                        for gg in range(gsz):
                            pw2 = list(pw) + [(sV, actdv[(gg, tp)])]
                            lastz = P.add(
                                "tensor", lambda e, gg=gg, tp=tp, zsl=zsl, sl2=sl2, gsz=gsz:
                                e.matmul(PSA[:, zsl * 512:(zsl + 1) * 512],
                                         wu2[:, sl2 * GSZ * 128 + gg * 128:sl2 * GSZ * 128 + (gg + 1) * 128],
                                         actb[:, gg * 1024 + tp * 512:gg * 1024 + (tp + 1) * 512],
                                         start=(gg == 0), stop=(gg == gsz - 1)),
                                waits=pw2, inc=sT if gg == gsz - 1 else None)
                            pw = ()
                        if gi == 0:
                            zv = P.add("vector", lambda e, r=r, tp=tp, zsl=zsl:
                                       e.tensor_copy(Zsb[:, r * 1024 + tp * 512:r * 1024 + (tp + 1) * 512],
                                                     PSA[:, zsl * 512:(zsl + 1) * 512]),
                                       waits=[(sT, lastz)], inc=sV)
                        elif gi == NGRP - 1:
                            zv = P.add("vector", lambda e, r=r, tp=tp, zsl=zsl:
                                       e.tensor_tensor(Zbf[:, r * 1024 + tp * 512:r * 1024 + (tp + 1) * 512],
                                                       Zsb[:, r * 1024 + tp * 512:r * 1024 + (tp + 1) * 512],
                                                       PSA[:, zsl * 512:(zsl + 1) * 512], ALU.add),
                                       waits=[(sT, lastz), (sV, zver.get((r, tp), 0))], inc=sV)
                            zfin[(r, tp)] = zv
                        else:
                            zv = P.add("vector", lambda e, r=r, tp=tp, zsl=zsl:
                                       e.tensor_tensor(Zsb[:, r * 1024 + tp * 512:r * 1024 + (tp + 1) * 512],
                                                       Zsb[:, r * 1024 + tp * 512:r * 1024 + (tp + 1) * 512],
                                                       PSA[:, zsl * 512:(zsl + 1) * 512], ALU.add),
                                       waits=[(sT, lastz), (sV, zver.get((r, tp), 0))], inc=sV)
                        zrel[zsl] = [(sV, zv)]
                        zver[(r, tp)] = zv
                    wrel2[sl2] = [(sT, lastz)]
                z_prev_group[0] = lastz
            return emit_z

        pending_z = None
        for gi, (g0, g1) in enumerate(groups):
            actdv = {}
            for g in range(g0, g1):
                sl = g % 2
                ssem = dG0 if sl == 0 else dG1
                if g < 2:
                    wld = v1_pre[g]
                else:
                    wsrc = v1_h.ap()[g].rearrange("a r p c -> p a r c")
                    wld = P.add("gpsimd", lambda e, wsrc=wsrc, sl=sl:
                                e.dma_start(wv1[:, sl * 2048:(sl + 1) * 2048].rearrange("p (a r c) -> p a r c", a=2, c=128), wsrc),
                                waits=wrel1[sl], inc=ssem, inck=16)
                wrel1[sl] = []
                mml = {}
                for a in range(2):
                    pw = ([(ssem, wld), (sS, pbf_done)] + hrel.get(HS[a * 2], [])
                          + hrel.get(HS[a * 2 + 1], []))
                    for r in range(NR):
                        for tp in range(2):
                            hs = HS[a * 2 + tp]
                            mml[(a, tp)] = P.add(
                                "tensor", lambda e, a=a, r=r, tp=tp, hs=hs, sl=sl:
                                e.matmul(PSA[:, hs:hs + 512],
                                         wv1[:, sl * 2048 + (a * NR + r) * 128:
                                             sl * 2048 + (a * NR + r + 1) * 128],
                                         Pbf[:, r * 1024 + tp * 512:r * 1024 + (tp + 1) * 512],
                                         start=(r == 0), stop=(r == NR - 1)),
                                waits=pw,
                                inc=sT if r == NR - 1 else None)
                            pw = ()
                wrel1[sl] = [(sT, mml[(1, 1)])]
                if g == g0 and pending_z is not None:
                    pending_z()
                    pending_z = None
                sgs = {}
                ubs = {}
                for tp in range(2):
                    ssl = (g * 2 + tp) % 2
                    sgs[tp] = P.add("scalar", lambda e, g=g, tp=tp, ssl=ssl:
                                    e.activation(sgst[:, ssl * 512:(ssl + 1) * 512],
                                                 PSA[:, HS[tp]:HS[tp] + 512], AF.Silu,
                                                 bias=b1t[:, g:g + 1]),
                                    waits=[(sT, mml[(0, tp)])] + sgrel[ssl], inc=sS)
                    hrel[HS[tp]] = [(sS, sgs[tp])]
                for tp in range(2):
                    ssl = (g * 2 + tp) % 2
                    ubs[tp] = P.add("scalar", lambda e, g=g, tp=tp, ssl=ssl:
                                    e.activation(ubst[:, ssl * 512:(ssl + 1) * 512],
                                                 PSA[:, HS[2 + tp]:HS[2 + tp] + 512], AF.Identity,
                                                 bias=b1t[:, NI + g:NI + g + 1]),
                                    waits=[(sT, mml[(1, tp)])] + ubrel[ssl], inc=sS)
                    hrel[HS[2 + tp]] = [(sS, ubs[tp])]
                for tp in range(2):
                    ssl = (g * 2 + tp) % 2
                    dv = P.add("vector", lambda e, g=g, g0=g0, tp=tp, ssl=ssl:
                               e.tensor_tensor(
                                   actb[:, (g - g0) * 1024 + tp * 512:(g - g0) * 1024 + (tp + 1) * 512],
                                   sgst[:, ssl * 512:(ssl + 1) * 512],
                                   ubst[:, ssl * 512:(ssl + 1) * 512], ALU.mult),
                               waits=[(sS, sgs[tp]), (sS, ubs[tp]), (sT, z_prev_group[0])], inc=sV)
                    sgrel[ssl] = [(sV, dv)]
                    ubrel[ssl] = [(sV, dv)]
                    actdv[(g - g0, tp)] = dv
            pending_z = make_z(gi, g0, g1, actdv)
        pending_z()
        v2_pre = {}
        for oc in range(4):
            wsrc = v2_h.ap()[oc].rearrange("r p c -> p r c")
            ssem = dG0 if oc % 2 == 0 else dG1
            v2_pre[oc] = P.add("gpsimd", lambda e, wsrc=wsrc, oc=oc:
                               e.dma_start(wbufD[:, 4096 + oc * 1024:4096 + (oc + 1) * 1024]
                                           .rearrange("p (r c) -> p r c", c=128), wsrc),
                               waits=wrelD[0] + wrelD[1], inc=ssem, inck=16)
        sD2.close()

        xfE = Pbf
        sD3 = contextlib.ExitStack()
        es.enter_context(sD3)
        sqD = sb("sqD", [128, 2 * 1024], BF16, sD3)
        wrel3 = {i: [] for i in range(4)}
        psrel3 = {0: [], 1: []}
        sqrel3 = {0: [], 1: []}
        st3_cell = [None]

        def d3_mm(oc):
            sl4 = oc % 4
            sl = oc % 2
            ssem = dG0 if sl == 0 else dG1
            if oc < 4:
                wld = v2_pre[oc]
            else:
                wsrc = v2_h.ap()[oc].rearrange("r p c -> p r c")
                wld = P.add("gpsimd", lambda e, wsrc=wsrc, sl4=sl4:
                            e.dma_start(wbufD[:, 4096 + sl4 * 1024:4096 + (sl4 + 1) * 1024]
                                        .rearrange("p (r c) -> p r c", c=128), wsrc),
                            waits=wrel3[sl4], inc=ssem, inck=16)
            wrel3[sl4] = []
            pb = MMS[sl]
            pw = [(ssem, wld)] + psrel3[sl]
            mml = None
            for r in range(NR):
                rw = list(pw) + ([(sV, zfin[(r, 0)]), (sV, zfin[(r, 1)])] if oc == 0 else [])
                for (p0, p1) in _pieces(0, 1024):
                    mml = P.add("tensor", lambda e, r=r, p0=p0, p1=p1, pb=pb, sl4=sl4:
                                e.matmul(PSA[:, pb + p0:pb + p1],
                                         wbufD[:, 4096 + sl4 * 1024 + r * 128:4096 + sl4 * 1024 + (r + 1) * 128],
                                         Zbf[:, r * 1024 + p0:r * 1024 + p1],
                                         start=(r == 0), stop=(r == NR - 1)),
                                waits=rw,
                                inc=sT if (r == NR - 1 and p1 == 1024) else None)
                    rw = ()
                pw = ()
            wrel3[sl4] = [(sT, mml)]
            return mml

        def d3_post(oc, mml):
            sl = oc % 2
            pb = MMS[sl]
            ev = P.add("scalar", lambda e, oc=oc, pb=pb:
                       e.activation(slab0[:, oc * 1024:(oc + 1) * 1024], PSA[:, pb:pb + 1024],
                                    AF.Identity, bias=b2t[:, oc:oc + 1]),
                       waits=[(sT, mml)], inc=sS)
            psrel3[sl] = [(sS, ev)]
            sqs = sqD[:, sl * 1024:(sl + 1) * 1024]
            sq = P.add("scalar", lambda e, sqs=sqs, oc=oc:
                       e.activation(sqs, slab0[:, oc * 1024:(oc + 1) * 1024], AF.Square),
                       waits=sqrel3[sl], inc=sS)
            sqrel3[sl] = []
            stl = None
            for pi, (p0, p1) in enumerate(_pieces(0, 1024)):
                sumtgt = PSA[0:1, p0:p1]
                sqtgt = PSA[0:1, STATS:STATS + 512] if pi == 0 else PSC[0:1, 0:512]
                P.add("tensor", lambda e, oc=oc, p0=p0, p1=p1, sumtgt=sumtgt:
                      e.matmul(sumtgt, onc[:],
                               slab0[:, oc * 1024 + p0:oc * 1024 + p1],
                               start=(oc == 0), stop=(oc == NF - 1)),
                      waits=[(sS, ev)] + ([(sS, gC2)] if oc == 0 else []), inc=None)
                stl = P.add("tensor", lambda e, sqs=sqs, p0=p0, p1=p1, sqtgt=sqtgt, oc=oc:
                            e.matmul(sqtgt, onc[:], sqs[:, p0:p1],
                                     start=(oc == 0), stop=(oc == NF - 1)),
                            waits=[(sS, sq)], inc=sT)
            st3_cell[0] = stl
            sqrel3[sl] = [(sT, stl)]

        prevd3 = None
        for oc in range(NF):
            mm = d3_mm(oc)
            if prevd3 is not None:
                d3_post(*prevd3)
            prevd3 = (oc, mm)
        d3_post(*prevd3)
        st3_last = st3_cell[0]
        x1_pre = {}
        for oc in range(4):
            asem = dL0 if oc % 2 == 0 else dL1
            x1_pre[oc] = P.add("sync" if oc % 2 == 0 else "scalar",
                               lambda e, oc=oc:
                               e.dma_start(xfE[:, oc * 1024:(oc + 1) * 1024], x1_d.ap()[oc]),
                               waits=list(x1_store) + wrel1[0] + wrel1[1],
                               inc=asem, inck=16)
        P.flush()
        sD3.close()
        sD.close()

        if DEV_STOP == "D":
            _dump(nc, P, dG, [("slab0", slab0)])
            return nc

        gE = ln_rows(st3_last)
        sE = contextlib.ExitStack()
        es.enter_context(sE)
        scE = sb("scE", [128, 4 * 1024], F32, sE)
        scEb = sb("scEb", [128, 4 * 1024], BF16, sE)
        orel = {0: [], 1: [], 2: [], 3: []}
        xrelE = {0: [], 1: [], 2: [], 3: []}

        def e_front(oc):
            sl4 = oc % 4
            asem = dL0 if oc % 2 == 0 else dL1
            if oc < 4:
                xv = x1_pre[oc]
            else:
                xv = P.add("sync" if oc % 2 == 0 else "scalar",
                           lambda e, oc=oc, sl4=sl4:
                           e.dma_start(xfE[:, sl4 * 1024:(sl4 + 1) * 1024], x1_d.ap()[oc]),
                           waits=xrelE[sl4], inc=asem, inck=16)
            sb_ = scEb[:, sl4 * 1024:(sl4 + 1) * 1024]
            P.add("vector", lambda e, sb_=sb_, oc=oc:
                  e.tensor_tensor(sb_, slab0[:, oc * 1024:(oc + 1) * 1024], grR[:], ALU.mult),
                  waits=[(sS, gE)], inc=sV)
            P.add("vector", lambda e, sb_=sb_: e.tensor_tensor(sb_, sb_, grMR[:], ALU.subtract), inc=sV)
            return (oc, sl4, sb_, xv, asem)

        def e_back(st):
            oc, sl4, sb_, xv, asem = st
            s0 = scE[:, sl4 * 1024:(sl4 + 1) * 1024]
            d3 = P.add("vector", lambda e, s0=s0, sb_=sb_, sl4=sl4:
                       e.tensor_tensor(s0, sb_, xfE[:, sl4 * 1024:(sl4 + 1) * 1024], ALU.add),
                       waits=[(asem, xv)] + orel[sl4], inc=sV)
            xrelE[sl4] = [(sV, d3)]
            osem = dG0 if oc % 2 == 0 else dG1
            sv_ = P.add("gpsimd",
                        lambda e, s0=s0, oc=oc:
                        e.dma_start(out_h.ap()[oc * 128:(oc + 1) * 128, :], s0),
                        waits=[(sV, d3)], inc=osem, inck=16)
            orel[sl4] = [(osem, sv_)]

        prevE = None
        for oc in range(NF):
            st = e_front(oc)
            if prevE is not None:
                e_back(prevE)
            prevE = st
        e_back(prevE)
        P.add("sync", lambda e: e.nop(), waits=[(dG0, dG0.n), (dG1, dG1.n)])
        P.flush()
        sE.close()
        sC.close()

    return nc


def _dump(nc, P, dG, tensors):
    for name, t in tensors:
        h = nc.dram_tensor("dump_" + name, [int(s) for s in t.shape], F32,
                           kind="ExternalOutput")
        P.add("gpsimd", lambda e, h=h, t=t: e.dma_start(h.ap(), t[:]), inc=dG, inck=16)
    P.add("sync", lambda e: e.nop(), waits=[(s, s.n) for s in P.dma_sems])
    P.flush()


def _host_shared(inputs):
    wq = np.asarray(inputs["L0_attn_wq"], np.float32)
    wk = np.asarray(inputs["L0_attn_wk"], np.float32)
    wv = np.asarray(inputs["L0_attn_wv"], np.float32)
    wo = np.asarray(inputs["L0_attn_wo"], np.float32)
    U1 = np.asarray(inputs["L0_ffn_U1"], np.float32)
    V1 = np.asarray(inputs["L0_ffn_V1"], np.float32)
    U2 = np.asarray(inputs["L0_ffn_U2"], np.float32)
    V2 = np.asarray(inputs["L0_ffn_V2"], np.float32)
    b1 = np.asarray(inputs["L0_ffn_b1"], np.float32)
    b2 = np.asarray(inputs["L0_ffn_b2"], np.float32)

    d = {}
    d["wq_b"] = _blk(wq, NF, NQ)
    d["wk_b"] = _blk(wk, NF, NKV)
    d["wv_b"] = _blk(wv, NF, NKV)
    d["wo_b"] = _blk(wo, NF, NF)
    d["u1_b"] = _blk(U1, NF, NR)
    v1b = _blk(V1, NR, NH2)
    d["v1_b"] = np.ascontiguousarray(np.stack([v1b[:NI], v1b[NI:]], axis=1))
    u2b = _blk(U2, NI, NR)
    order = []
    for (i0, i1) in _ffn_groups():
        for r in range(NR):
            for i in range(i0, i1):
                order.append(u2b[r, i])
    d["u2_b"] = np.ascontiguousarray(np.stack(order))
    d["v2_b"] = _blk(V2, NR, NF)
    d["b1_t"] = np.ascontiguousarray(b1.reshape(NH2, 128).T)
    d["b2_t"] = np.ascontiguousarray(b2.reshape(NF, 128).T)
    ii = np.arange(128)
    bf16 = ml_dtypes.bfloat16
    d["diag_m"] = (ii[:, None] <= ii[None, :]).astype(bf16)
    d["win_m"] = (ii[:, None] > ii[None, :]).astype(bf16)
    ps = np.zeros((128, 128), np.float32)
    ps[np.arange(64) + 64, np.arange(64)] = 1.0
    ps[np.arange(64), np.arange(64) + 64] = 1.0
    d["pswap"] = ps.astype(bf16)
    d["ident"] = np.eye(128, dtype=bf16)
    d["ones_row"] = np.ones((1, 128), np.float32)
    d["ones_col"] = np.ones((128, 1), bf16)
    return d


def _host_core(inputs, b, h):
    x = np.asarray(inputs["L0_attn_x"], np.float32)[b]
    own = slice(h * TOWN, (h + 1) * TOWN)
    oth = slice((1 - h) * TOWN, (2 - h) * TOWN)
    xT = x.T
    d = {}
    d["x_rot"] = np.ascontiguousarray(np.concatenate([xT[:, oth], xT[:, own]], axis=1))
    pos = np.arange(TALL)
    pos_rot = np.concatenate([pos[oth], pos[own]]).astype(np.float32)
    inv = THETA ** (-np.arange(64, dtype=np.float32) / 64.0)
    ang = inv[:, None] * pos_rot[None, :]
    d["cc_t"] = np.ascontiguousarray(
        np.concatenate([np.cos(ang), np.cos(ang)], axis=0)).astype(ml_dtypes.bfloat16)
    d["ss_t"] = np.ascontiguousarray(
        np.concatenate([-np.sin(ang), np.sin(ang)], axis=0)).astype(ml_dtypes.bfloat16)
    d["bias_off"] = np.full((1, 32), 0.0 if h == 1 else NEG, np.float32)
    return d


def kernel(**inputs):
    shared = _host_shared(inputs)
    cores = []
    pairs = [(b, h) for b in range(4) for h in range(2)]
    for b, h in pairs:
        m = dict(shared)
        m.update(_host_core(inputs, b, h))
        cores.append(m)
    nc = build_program()
    res = run_bass_kernel_spmd(nc, cores, core_ids=list(range(8)))
    out = np.empty((4, TALL, HID), np.float32)
    for i, (b, h) in enumerate(pairs):
        out[b, h * TOWN:(h + 1) * TOWN, :] = np.asarray(res.results[i]["out"]).T
    return out
